# version 06
# baseline (speedup 1.0000x reference)
import numpy as np
import ml_dtypes
from contextlib import ExitStack
import concourse.bass as bass
import concourse.mybir as mybir
from concourse.bass_utils import run_bass_kernel_spmd

F32 = mybir.dt.float32
BF16 = mybir.dt.bfloat16
I32 = mybir.dt.int32
AF = mybir.ActivationFunctionType
ALU = mybir.AluOpType
AXX = mybir.AxisListType.X
BFNP = ml_dtypes.bfloat16

NCORES = 8
NB = 2
T = 2048
D = 2048
NT = T // 128
H = 16
DH = 128
EPS = 1e-5
A0_SCALE = DH ** -0.5
A0_SHIFT = 16.0
FI = 8192
FCOL = {"eps": 0, "nshift": 1, "lns1": 2, "zero": 3}
FCOL_VALS = {"eps": EPS, "nshift": -A0_SHIFT, "lns1": float(np.log(128.0 ** -0.5)), "zero": 0.0}
DBG_BAR_F1 = False
DBG_BAR_F2 = False


class Sem:
    __slots__ = ("h", "n")

    def __init__(self, h):
        self.h = h
        self.n = 0


class Ctx:
    ENG = ("tensor", "vector", "scalar", "gpsimd", "sync")

    def __init__(self, nc, es):
        self.nc = nc
        self.es = es
        self.q = {e: [] for e in self.ENG}
        self.trace = {e: [] for e in self.ENG}
        self.sems = {}
        for k in ("x", "w", "m", "v", "a", "c", "y", "d0", "d1", "d2", "d3"):
            self.sems[k] = Sem(es.enter_context(nc.semaphore("s_" + k)))
        self.sx = self.sems["x"]
        self.sw = self.sems["w"]
        self.sm = self.sems["m"]
        self.sv = self.sems["v"]
        self.sa = self.sems["a"]
        self.sc = self.sems["c"]
        self.sy = self.sems["y"]
        self.sd = [self.sems["d%d" % i] for i in range(4)]
        self._semkey = {id(s): k for k, s in self.sems.items()}
        self.dram = {}
        self.cons = {}
        self.bufs = {}
        self.war = {}
        self._u = 0

    def sb(self, stack, name, shape, dt):
        self._u += 1
        return stack.enter_context(self.nc.sbuf_tensor(f"{name}_{self._u}", shape, dt))

    def ps(self, stack, name, shape, dt):
        self._u += 1
        return stack.enter_context(self.nc.psum_tensor(f"{name}_{self._u}", shape, dt))

    def w(self, eng, sem, val):
        if val > 0:
            self.q[eng].append(lambda e, s=sem.h, v=val: e.wait_ge(s, v))
            self.trace[eng].append(("w", self._semkey[id(sem)], val))

    SELF_SYNC = {"vector": "v", "scalar": "a", "gpsimd": "w"}

    def do(self, eng, meth, *a, inc=None, n=1, lp=False, **kw):
        nc = self.nc
        if eng in self.SELF_SYNC and meth != "dma_start":
            own = self.sems[self.SELF_SYNC[eng]]
            self.w(eng, own, own.n)
            if inc is None:
                inc = own
        if inc is not None:
            inc.n += n

            def f(e, m=meth, a=a, kw=kw, s=inc.h, nn=n, lp=lp):
                if lp:
                    with nc.allow_low_precision("bf16 path by design"):
                        getattr(e, m)(*a, **kw).then_inc(s, nn)
                else:
                    getattr(e, m)(*a, **kw).then_inc(s, nn)
            f.meta = meth
            self.trace[eng].append(("i", self._semkey[id(inc)], n))
        else:
            def f(e, m=meth, a=a, kw=kw, lp=lp):
                if lp:
                    with nc.allow_low_precision("bf16 path by design"):
                        getattr(e, m)(*a, **kw)
                else:
                    getattr(e, m)(*a, **kw)
            f.meta = meth
            self.trace[eng].append(("i", None, 0))
        self.q[eng].append(f)
        return inc.n if inc is not None else None

    def barrier(self):
        for e in self.ENG:
            for s in self.sems.values():
                self.w(e, s, s.n)

    def check(self):
        pos = {e: 0 for e in self.ENG}
        cnt = {k: 0 for k in self.sems}
        total = sum(len(t) for t in self.trace.values())
        done = 0
        progress = True
        while progress:
            progress = False
            for e in self.ENG:
                t = self.trace[e]
                while pos[e] < len(t):
                    kind, key, val = t[pos[e]]
                    if kind == "w" and cnt[key] < val:
                        break
                    if kind == "i" and key is not None:
                        cnt[key] += val
                    pos[e] += 1
                    done += 1
                    progress = True
        if done != total:
            stuck = {e: (pos[e], len(self.trace[e]), self.trace[e][pos[e]]
                         if pos[e] < len(self.trace[e]) else None) for e in self.ENG}
            raise RuntimeError(f"DEADLOCK in schedule: sems={cnt} stuck={stuck}")

    def play(self):
        self.check()

        def runner(eng, fns):
            for i, f in enumerate(fns):
                try:
                    f(eng)
                except Exception as ex:
                    raise RuntimeError(
                        "emit failed at %s[%d]: %s" % (eng, i, getattr(f, "meta", "wait"))
                    ) from ex

        with self.nc.Block() as blk:
            for e in self.ENG:
                if self.q[e]:
                    getattr(blk, e)(lambda eng, fns=tuple(self.q[e]): runner(eng, fns))



def load_const_sb(C, ph, name, dram_key, shape, dt_, cast=False):
    t = C.sb(ph, "pc_" + name, shape, dt_)[:]
    C.do("gpsimd" if cast else "sync", "dma_start", inc=C.sc, n=16,
         out=t, in_=C.dram[dram_key])
    val = C.sc.n
    for e in ("tensor", "vector", "scalar"):
        C.w(e, C.sc, val)
    C.cons[name] = t
    return t


def build_bc_tile(C, ph, name, row_dram_key, width):
    nc = C.nc
    sm, sv = C.sm, C.sv
    row = C.sb(ph, "bcrow_" + name, [1, width], F32)[:]
    tile = C.sb(ph, "bctile_" + name, [128, width], F32)[:]
    C.do("sync", "dma_start", inc=C.sc, n=16, out=row, in_=C.dram[row_dram_key])
    rdy = C.sc.n
    with nc.psum_tensor("psbc_%d" % C._u, [128, 512], F32) as psI:
        C._u += 1
        C.w("tensor", C.sc, rdy)
        for j in range((width + 511) // 512):
            w_ = min(512, width - j * 512)
            C.w("tensor", sv, sv.n)
            C.do("tensor", "matmul", psI[:, 0:w_], C.cons["onesrf"],
                 row[:, j * 512:j * 512 + w_], start=True, stop=True, inc=sm)
            C.w("vector", sm, sm.n)
            C.do("vector", "tensor_copy", out=tile[:, j * 512:j * 512 + w_],
                 in_=psI[:, 0:w_], inc=sv)
    C.cons[name] = tile
    C.war[name + "_sv"] = sv.n
    return tile


def ln_cols(C, cols, base, ssq_done):
    sv, sa = C.sv, C.sa
    c = lambda j: cols[:, base + j:base + j + 1]
    C.w("vector", sa, ssq_done)
    C.do("vector", "tensor_scalar", c(2), c(0), 1.0 / D, None, ALU.mult, inc=sv)
    C.do("vector", "tensor_tensor", c(3), c(2), c(2), ALU.mult, inc=sv)
    C.do("vector", "tensor_scalar", c(4), c(1), 1.0 / D, c(3), ALU.mult, ALU.subtract, inc=sv)
    C.w("scalar", sv, sv.n)
    C.do("scalar", "activation", c(5), c(4), AF.Sqrt, bias=C.cons["fcol"][:, FCOL["eps"]:FCOL["eps"] + 1], inc=sa)
    C.w("vector", sa, sa.n)
    C.do("vector", "reciprocal", c(6), c(5), inc=sv)
    return c(2), c(6)


def emit_ln_to_xnT(C, x_src, xnT):
    sx, sm, sv, sa = C.sx, C.sm, C.sv, C.sa
    nc = C.nc
    ph = ExitStack()
    xt2 = C.sb(ph, "lnxt", [128, 2 * D], F32)
    xnb2 = C.sb(ph, "lnxnb", [128, 2 * D], BF16)
    sqs = C.sb(ph, "lnsqs", [128, D], BF16)
    ptA = C.ps(ph, "lnptA", [128, 1024], BF16)
    cols = C.bufs["cols"]
    cons_done = [0, 0]
    xnb_used = [0, 0]
    ptA_war = 0
    for i in range(NT):
        bi = i % 2
        xt = xt2[:, bi * D:(bi + 1) * D]
        xnb = xnb2[:, bi * D:(bi + 1) * D]
        C.w("sync", sv, cons_done[bi])
        C.do("sync", "dma_start", inc=C.sd[bi], n=16, out=xt,
             in_=x_src[i * 128:(i + 1) * 128, :])
        xload = C.sd[bi].n
        base = bi * 8
        c = lambda j: cols[:, base + j:base + j + 1]
        C.w("vector", C.sd[bi], xload)
        C.do("vector", "reduce_sum", c(0), xt, axis=AXX, inc=sv)
        C.w("scalar", C.sd[bi], xload)
        C.w("scalar", sv, cons_done[bi])
        C.do("scalar", "activation", sqs[:], xt, AF.Square, accum_out=c(1), inc=sa)
        mu, rstd = ln_cols(C, cols, base, sa.n)
        C.w("vector", sm, xnb_used[bi])
        C.do("vector", "tensor_scalar", xnb, xt, mu, rstd, ALU.subtract, ALU.mult, inc=sv)
        cons_done[bi] = sv.n
        xnb_done = sv.n
        for g in range(2):
            C.w("tensor", sv, xnb_done)
            C.w("tensor", sv, ptA_war)
            for j in range(8):
                dc = g * 8 + j
                C.do("tensor", "transpose", ptA[:, j * 128:(j + 1) * 128],
                     xnb[:, dc * 128:(dc + 1) * 128], C.cons["idn"],
                     inc=(sm if j == 7 else None))
            C.w("vector", sm, sm.n)
            dst = xnT.rearrange("p (dc t) -> p dc t", dc=16)[:, g * 8:(g + 1) * 8,
                                                            i * 128:(i + 1) * 128]
            C.do("vector", "tensor_copy", out=dst,
                 in_=ptA[:].rearrange("p (j t) -> p j t", j=8), inc=sv)
            ptA_war = sv.n
        xnb_used[bi] = sm.n
    ph.close()
    return sv.n


def emit_post_ln_residual(C, ybf_tile, x_src_tile, x_dst_tile, slot, y_ready_val):
    sx, sv, sa = C.sx, C.sv, C.sa
    b = C.bufs
    xt = b["xt3"][:, slot * D:(slot + 1) * D]
    cols, sqs = b["cols2"], b["sqs2"]
    base = slot * 8
    c = lambda j: cols[:, base + j:base + j + 1]
    sd = C.sd[2 + slot]
    C.w("sync", sd, C.war.get("res_store%d" % slot, 0))
    C.do("sync", "dma_start", inc=sd, n=16, out=xt, in_=x_src_tile)
    xload = sd.n
    C.w("vector", sv, y_ready_val)
    C.do("vector", "reduce_sum", c(0), ybf_tile, axis=AXX, inc=sv)
    C.w("scalar", sv, y_ready_val)
    C.w("scalar", sv, C.war.get("res_cons%d" % slot, 0))
    C.do("scalar", "activation", sqs, ybf_tile, AF.Square, accum_out=c(1), inc=sa)
    C.war["res_sq"] = sa.n
    mu, rstd = ln_cols(C, cols, base, sa.n)
    C.do("vector", "tensor_scalar", c(7), mu, rstd, -1.0, ALU.mult, ALU.mult, inc=sv)
    C.w("vector", sd, xload)
    C.do("vector", "tensor_scalar", xt, xt, c(7), None, ALU.add, inc=sv)
    C.do("vector", "scalar_tensor_tensor", xt, ybf_tile, rstd, xt, ALU.mult, ALU.add, inc=sv)
    C.war["res_cons%d" % slot] = sv.n
    C.w("sync", sv, sv.n)
    C.do("sync", "dma_start", inc=sd, n=16, out=x_dst_tile, in_=xt)
    C.war["res_store%d" % slot] = sd.n


def emit_block_a(C, b, x_in, x_out):
    nc = C.nc
    sx, sw, sm, sv, sa = C.sx, C.sw, C.sm, C.sv, C.sa
    cons = C.cons

    ph0 = ExitStack()
    xnT = C.sb(ph0, "xnT", [128, 16 * 2048], BF16)[:]
    load_const_sb(C, ph0, "ropeC", "c_ropeC", [64, 2048], BF16)
    load_const_sb(C, ph0, "ropeS", "c_ropeS", [64, 2048], BF16)
    C.barrier()
    xnt_ready = emit_ln_to_xnT(C, x_in[b], xnT)

    C.barrier()
    ph = ExitStack()
    wqkv_sb = C.sb(ph, "a2w", [128, 3 * 2048], BF16)
    qkv_sb = C.sb(ph, "a2qkv", [128, 3 * 2048], BF16)
    vtok = C.sb(ph, "a2vtok", [128, 2048], BF16)
    rrot = C.sb(ph, "a2rrot", [64, 2048], BF16)
    PT = C.sb(ph, "a2PT", [128, 16 * 512], BF16)
    drec = C.sb(ph, "a2drec", [1, 2048], F32)
    bcb = C.sb(ph, "a2bcb", [128, 512], F32)
    attnh2 = C.sb(ph, "a2attnh", [128, 2 * 2048], BF16)
    psA = C.ps(ph, "a2psA", [128, 512], F32)
    psS = C.ps(ph, "a2psS", [128, 2 * 1024], F32)
    psD = C.ps(ph, "a2psD", [1, 512], F32)
    psO = C.ps(ph, "a2psO", [128, 512], F32)
    ptA = C.ps(ph, "a2ptV", [128, 1024], BF16)

    qT = qkv_sb[:, 0:2048]
    kT = qkv_sb[:, 2048:4096]
    vT = qkv_sb[:, 4096:6144]
    w_used = 0
    attn_stored = [0, 0]
    psA_war = 0
    qkv_tensor_done = 0
    vtok_tensor_done = 0
    attn_evac = 0
    exp_done = [0, 0]
    for h in range(H):
        hb = h % 2
        C.w("gpsimd", sm, w_used)
        for t3 in range(3):
            src = C.dram["wqkv"][t3, h].rearrange("p dc f -> p (dc f)")
            C.do("gpsimd", "dma_start", inc=sw, n=16,
                 out=wqkv_sb[:, t3 * 2048:(t3 + 1) * 2048], in_=src)
        w_ready = sw.n
        for t3 in range(3):
            for qc in range(4):
                if t3 == 0 and qc == 0:
                    C.w("tensor", sw, w_ready)
                    C.w("tensor", sv, xnt_ready)
                C.w("tensor", sv, psA_war)
                for dc in range(16):
                    C.do("tensor", "matmul", psA[:],
                         wqkv_sb[:, t3 * 2048 + dc * 128:t3 * 2048 + (dc + 1) * 128],
                         xnT[:, dc * 2048 + qc * 512:dc * 2048 + (qc + 1) * 512],
                         start=(dc == 0), stop=(dc == 15),
                         inc=(sm if dc == 15 else None))
                mm_done = sm.n
                C.w("vector", sm, mm_done)
                if t3 == 0 and qc == 0:
                    C.w("vector", sm, qkv_tensor_done)
                C.do("vector", "tensor_scalar",
                     qkv_sb[:, t3 * 2048 + qc * 512:t3 * 2048 + (qc + 1) * 512],
                     psA[:], C.cons["bqkv"][:, h * 3 + t3:h * 3 + t3 + 1], None,
                     ALU.add, inc=sv)
                psA_war = sv.n
        w_used = sm.n
        qkv_done = sv.n
        for tt in (qT, kT):
            C.do("vector", "tensor_copy", out=rrot[0:32, :], in_=tt[32:64, :], inc=sv)
            C.do("vector", "tensor_copy", out=rrot[32:64, :], in_=tt[0:32, :], inc=sv)
            C.do("vector", "tensor_tensor", rrot[0:64, :], rrot[0:64, :],
                 cons["ropeS"][0:64, :], ALU.mult, inc=sv)
            C.do("vector", "tensor_tensor", tt[0:64, :], tt[0:64, :],
                 cons["ropeC"][0:64, :], ALU.mult, inc=sv)
            C.do("vector", "tensor_tensor", tt[0:64, :], tt[0:64, :],
                 rrot[0:64, :], ALU.add, inc=sv)
        rope_done = sv.n
        for g in range(2):
            C.w("tensor", sv, qkv_done)
            C.w("tensor", sv, C.war.get("a2ptV", 0))
            for j in range(8):
                kt = g * 8 + j
                C.do("tensor", "transpose", ptA[:, j * 128:(j + 1) * 128],
                     vT[:, kt * 128:(kt + 1) * 128], cons["idn"],
                     inc=(sm if j == 7 else None))
            C.w("vector", sm, sm.n)
            C.w("vector", sm, vtok_tensor_done)
            C.do("vector", "tensor_copy", out=vtok[:, g * 1024:(g + 1) * 1024],
                 in_=ptA[:], inc=sv)
            C.war["a2ptV"] = sv.n
        vtok_done = sv.n
        for qc in range(4):
            for sg in range(8):
                sb_ = sg % 2
                if sg == 0:
                    C.w("tensor", sv, rope_done)
                    C.w("tensor", sv, attn_evac)
                C.w("tensor", sa, exp_done[sb_])
                for j in range(2):
                    kt = sg * 2 + j
                    C.do("tensor", "matmul",
                         psS[:, sb_ * 1024 + j * 512:sb_ * 1024 + (j + 1) * 512],
                         kT[:, kt * 128:(kt + 1) * 128],
                         qT[:, qc * 512:(qc + 1) * 512],
                         start=True, stop=True, inc=(sm if j == 1 else None))
                s_done = sm.n
                C.w("scalar", sm, s_done)
                C.do("scalar", "activation", PT[:, sg * 1024:(sg + 1) * 1024],
                     psS[:, sb_ * 1024:(sb_ + 1) * 1024], AF.Exp,
                     bias=C.cons["fcol"][:, FCOL["nshift"]:FCOL["nshift"] + 1],
                     scale=A0_SCALE, inc=sa)
                exp_done[sb_] = sa.n
            all_exp = sa.n
            qkv_tensor_done = sm.n + 16 + 16
            C.w("tensor", sa, all_exp)
            C.w("tensor", sv, vtok_done)
            if h == 0 and qc == 0:
                C.w("tensor", sx, C.war["consts_sx"])
            for kt in range(16):
                C.do("tensor", "matmul", psD[:], cons["onesc"],
                     PT[:, kt * 512:(kt + 1) * 512], start=(kt == 0), stop=(kt == 15),
                     inc=(sm if kt == 15 else None))
            den_done = sm.n
            for kt in range(16):
                C.do("tensor", "matmul", psO[:], vtok[:, kt * 128:(kt + 1) * 128],
                     PT[:, kt * 512:(kt + 1) * 512], start=(kt == 0), stop=(kt == 15),
                     inc=(sm if kt == 15 else None))
            pv_done = sm.n
            vtok_tensor_done = pv_done
            C.w("vector", sm, den_done)
            C.do("vector", "reciprocal", drec[:, qc * 512:(qc + 1) * 512], psD[:], inc=sv)
            rec_done = sv.n
            C.w("tensor", sv, rec_done)
            C.w("tensor", sv, psA_war)
            C.do("tensor", "matmul", psA[:], cons["onesrf"],
                 drec[:, qc * 512:(qc + 1) * 512], start=True, stop=True, inc=sm)
            bc_done = sm.n
            C.w("vector", sm, bc_done)
            C.do("vector", "tensor_copy", out=bcb[:], in_=psA[:], inc=sv)
            psA_war = sv.n
            C.w("vector", sm, pv_done)
            C.w("vector", C.sd[hb], attn_stored[hb])
            C.do("vector", "tensor_tensor",
                 attnh2[:, hb * 2048 + qc * 512:hb * 2048 + (qc + 1) * 512],
                 psO[:], bcb[:], ALU.mult, inc=sv)
            attn_evac = sv.n
        qkv_tensor_done = sm.n
        C.w("sync", sv, attn_evac)
        C.do("sync", "dma_start", inc=C.sd[hb], n=16, out=C.dram["attnT"][b, h],
             in_=attnh2[:, hb * 2048:(hb + 1) * 2048])
        attn_stored[hb] = C.sd[hb].n
    a2_end_sx = sx.n
    a2_end_d = [C.sd[0].n, C.sd[1].n]
    ph.close()
    ph0.close()

    emit_out_proj(C, b, C.dram["attnT"], C.dram["wout"], "bout", x_in, x_out)


def emit_out_proj(C, b, src_scr, wout_d, bias_key, x_in, x_out):
    nc = C.nc
    sx, sw, sm, sv, sa = C.sx, C.sw, C.sm, C.sv, C.sa
    cons = C.cons
    use_bias = bias_key is not None
    a2_end_sx = sx.n
    C.barrier()
    ph = ExitStack()
    if use_bias:
        build_bc_tile(C, ph, "boutbc", bias_key, 2048)
    C.bufs["xt3"] = C.sb(ph, "a3xt3", [128, 2 * D], F32)[:]
    C.bufs["sqs2"] = C.sb(ph, "a3sqs2", [128, D], BF16)[:]
    atsl = C.sb(ph, "a3atsl", [128, 16 * 512], BF16)
    wo2 = C.sb(ph, "a3wo", [128, 2 * 2048], BF16)
    ptmp = C.sb(ph, "a3ptmp", [128, 512], BF16)
    projtm = C.sb(ph, "a3projtm", [128, 4 * 2048], BF16)
    psP = C.ps(ph, "a3psP", [128, 2 * 512], F32)
    psT = C.ps(ph, "a3psT", [128, 512], BF16)
    atsl_war = 0
    wo_used = [0, 0]
    evac_sv = [0, 0]
    psT_war = 0
    projtm_used = 0
    for tc in range(4):
        C.w("scalar", sm, atsl_war)
        if tc == 0:
            C.w("scalar", sx, a2_end_sx)
            for _i in range(4):
                C.w("scalar", C.sd[_i], C.sd[_i].n)
        src = src_scr[b][:, :, tc * 512:(tc + 1) * 512].rearrange("h p q -> p h q")
        C.do("scalar", "dma_start", inc=C.sy, n=16,
             out=atsl.rearrange("p (h q) -> p h q", h=16), in_=src)
        atsl_ready = C.sy.n
        for dt in range(16):
            wb = dt % 2
            C.w("gpsimd", sm, wo_used[wb])
            C.do("gpsimd", "dma_start", inc=C.sd[wb], n=16,
                 out=wo2[:, wb * 2048:(wb + 1) * 2048], in_=wout_d[dt])
            wo_ready = C.sd[wb].n
            pb = dt % 2
            C.w("tensor", C.sd[wb], wo_ready)
            if dt == 0:
                C.w("tensor", C.sy, atsl_ready)
            C.w("tensor", sv, evac_sv[pb])
            for dc in range(16):
                C.do("tensor", "matmul", psP[:, pb * 512:(pb + 1) * 512],
                     wo2[:, wb * 2048 + dc * 128:wb * 2048 + (dc + 1) * 128],
                     atsl[:, dc * 512:(dc + 1) * 512],
                     start=(dc == 0), stop=(dc == 15), inc=(sm if dc == 15 else None))
            wo_used[wb] = sm.n
            proj_done = sm.n
            C.w("vector", sm, proj_done)
            C.do("vector", "tensor_copy", out=ptmp[:], in_=psP[:, pb * 512:(pb + 1) * 512],
                 inc=sv)
            evac_sv[pb] = sv.n
            ptmp_done = sv.n
            C.w("tensor", sv, ptmp_done)
            C.w("tensor", sv, psT_war)
            for j in range(4):
                C.do("tensor", "transpose", psT[:, j * 128:(j + 1) * 128],
                     ptmp[:, j * 128:(j + 1) * 128], cons["idn"],
                     inc=(sm if j == 3 else None))
            tt_done = sm.n
            C.w("vector", sm, tt_done)
            if dt == 0:
                C.w("vector", sv, projtm_used)
                C.w("vector", sa, C.war.get("res_sq", 0))
                C.w("vector", sv, C.war.get("boutbc_sv", 0))
            dst = projtm.rearrange("p (j n) -> p j n", j=4)[:, :, dt * 128:(dt + 1) * 128]
            if use_bias:
                C.do("vector", "scalar_tensor_tensor", dst,
                     psT.rearrange("p (j n) -> p j n", j=4), 1.0,
                     cons["boutbc"][:, dt * 128:(dt + 1) * 128].unsqueeze(1).broadcast_to(
                         [128, 4, 128]),
                     ALU.mult, ALU.add, inc=sv)
            else:
                C.do("vector", "tensor_copy", out=dst,
                     in_=psT.rearrange("p (j n) -> p j n", j=4), inc=sv)
            psT_war = sv.n
        proj_ready = sv.n
        for j in range(4):
            i = tc * 4 + j
            emit_post_ln_residual(C, projtm[:, j * 2048:(j + 1) * 2048],
                                  x_in[b][i * 128:(i + 1) * 128, :],
                                  x_out[b][i * 128:(i + 1) * 128, :],
                                  j % 2, proj_ready)
        projtm_used = sv.n
        atsl_war = sm.n
    ph.close()


L1S = 128 ** -0.5
CPAD = 16
KLAM, KLAMP, KBS2N, KBETA, KNLAM, KNGPS, KGT, KGC = range(8)


def emit_block_c(C, b, x_in, x_out):
    nc = C.nc
    sx, sw, sm, sv, sa, sy = C.sx, C.sw, C.sm, C.sv, C.sa, C.sy
    cons = C.cons

    ph0 = ExitStack()
    lamT = C.sb(ph0, "c_lamT", [16, 2048], F32)[:]
    lampT = C.sb(ph0, "c_lampT", [16, 2048], F32)[:]
    bs2nT = C.sb(ph0, "c_bs2nT", [16, 2048], F32)[:]
    betaT = C.sb(ph0, "c_betaT", [16, 2048], F32)[:]
    lnaT = C.sb(ph0, "c_lnaT", [16, 2048], F32)[:]
    alphaT = C.sb(ph0, "c_alphaT", [16, 2048], F32)[:]
    colstore = C.sb(ph0, "c_cols", [128, 16 * 128], F32)[:]
    col = lambda cc, k: colstore[:, cc * 128 + k * 16:cc * 128 + (k + 1) * 16]
    load_const_sb(C, ph0, "selF", "c_selF", [16, 2048], F32)
    build_bc_tile(C, ph0, "lnwbc", "c_lnw", 128)
    build_bc_tile(C, ph0, "lnbbc", "c_lnb", 128)
    phX = ExitStack()
    xnT = C.sb(phX, "xnTc", [128, 16 * 2048], BF16)[:]

    C.barrier()
    xnt_ready = emit_ln_to_xnT(C, x_in[b], xnT)

    C.barrier()
    ph = ExitStack()
    psZ = C.ps(ph, "c1psZ", [16, 2 * 512], F32)
    psC = C.ps(ph, "c1psC", [128, 128], F32)
    psG = C.ps(ph, "c1psG", [128, 16], F32)
    psZ_war = 0
    for qc in range(4):
        sl = slice(qc * 512, (qc + 1) * 512)
        if qc == 0:
            C.w("tensor", sv, xnt_ready)
        C.w("tensor", sa, psZ_war)
        for z, woff in ((0, 0), (1, 256)):
            for dc in range(16):
                C.do("tensor", "matmul", psZ[:, z * 512:(z + 1) * 512],
                     cons["wab"][:, woff + dc * 16:woff + (dc + 1) * 16],
                     xnT[:, dc * 2048 + qc * 512:dc * 2048 + (qc + 1) * 512],
                     start=(dc == 0), stop=(dc == 15), inc=(sm if dc == 15 else None))
        zdone = sm.n
        C.w("scalar", sm, zdone)
        C.do("scalar", "activation", alphaT[:, sl], psZ[:, 0:512], AF.Sigmoid,
             bias=cons["bab"][:, 0:1], inc=sa)
        C.do("scalar", "activation", betaT[:, sl], psZ[:, 512:1024], AF.Sigmoid,
             bias=cons["bab"][:, 1:2], inc=sa)
        psZ_war = sa.n
        C.do("scalar", "activation", lnaT[:, sl], alphaT[:, sl], AF.Ln, inc=sa)
        ln_done = sa.n
        C.w("vector", sa, ln_done)
        C.do("vector", "tensor_scalar", bs2nT[:, sl], betaT[:, sl],
             -(L1S * L1S), None, ALU.mult, inc=sv)
        for c in range(qc * 4, (qc + 1) * 4):
            csl = slice(c * 128, (c + 1) * 128)
            C.do("vector", "tensor_tensor_scan", lamT[:, csl], lnaT[:, csl],
                 lnaT[:, csl], 0.0, ALU.add, ALU.bypass, inc=sv)
            C.do("vector", "tensor_sub", lampT[:, csl], lamT[:, csl], lnaT[:, csl],
                 inc=sv)
        lam_done = sv.n
        for c in range(qc * 4, (qc + 1) * 4):
            csl = slice(c * 128, (c + 1) * 128)
            C.w("tensor", sv, lam_done)
            C.w("tensor", sv, C.war.get("c1_psC", 0))
            for k, tsrc in ((KLAM, lamT), (KLAMP, lampT), (KBS2N, bs2nT), (KBETA, betaT)):
                C.do("tensor", "transpose", psC[:, k * 16:(k + 1) * 16],
                     tsrc[:, csl], cons["idnf"][0:16, 0:16],
                     inc=(sm if k == KBETA else None))
            tdone = sm.n
            C.w("vector", sm, tdone)
            C.do("vector", "tensor_copy", out=colstore[:, c * 128:c * 128 + 64],
                 in_=psC[:, 0:64], inc=sv)
            C.war["c1_psC"] = sv.n
            C.do("vector", "tensor_scalar", col(c, KNLAM), col(c, KLAM), -1.0, None,
                 ALU.mult, inc=sv)
            C.w("scalar", sv, sv.n)
            C.do("scalar", "activation", col(c, KNGPS), col(c, KLAMP), AF.Exp,
                 bias=cons["fcol"][:, FCOL["lns1"]:FCOL["lns1"] + 1], inc=sa)
            C.do("scalar", "activation", col(c, KGT), col(c, KLAM), AF.Exp, inc=sa)
            C.w("vector", sa, sa.n)
            C.do("vector", "tensor_scalar", col(c, KNGPS), col(c, KNGPS), -1.0, None,
                 ALU.mult, inc=sv)
            C.w("tensor", sv, sv.n)
            C.w("tensor", sa, C.war.get("c1_psG", 0))
            for hh in range(16):
                C.do("tensor", "matmul", psG[:, hh:hh + 1],
                     cons["selF"][:, hh * 128:(hh + 1) * 128],
                     lamT[:, c * 128 + 127:c * 128 + 128],
                     start=True, stop=True, inc=(sm if hh == 15 else None))
            C.w("scalar", sm, sm.n)
            C.do("scalar", "activation", col(c, KGC), psG[:], AF.Exp, inc=sa)
            C.war["c1_psG"] = sa.n
    ph.close()

    C.barrier()
    ph = ExitStack()
    w4 = C.sb(ph, "c2w", [128, 4 * 2048], BF16)[:]
    CW = CPAD + 2048
    qraw = C.sb(ph, "c2qraw", [128, 3 * CW], BF16)[:]
    ctmp = C.sb(ph, "c2ctmp", [128, 2048], F32)[:]
    qkT = C.sb(ph, "c2qk", [128, 2 * 2048], BF16)[:]
    vsT = C.sb(ph, "c2vs", [128, 2048], BF16)[:]
    gT = C.sb(ph, "c2gT", [128, 2048], BF16)[:]
    vtm = C.sb(ph, "c2vtm", [128, 2048], BF16)[:]
    gat = C.sb(ph, "c2gat", [128, 2048], BF16)[:]
    psA = C.ps(ph, "c2psA", [128, 2 * 512], F32)
    ptC = C.ps(ph, "c2ptC", [128, 1024], BF16)
    qkvg = C.dram["qkvg_scr"]
    C.do("vector", "memset", qraw[:, 0:CPAD], 0.0)
    C.do("vector", "memset", qraw[:, CW:CW + CPAD], 0.0)
    C.do("vector", "memset", qraw[:, 2 * CW:2 * CW + CPAD], 0.0)
    w_used = 0
    psA_war_v = [0, 0]
    psA_war_a = [0, 0]
    store_done = {}
    gT_used = 0
    ptC_war = 0
    for h in range(H):
        C.w("gpsimd", sm, w_used)
        for t4 in range(4):
            C.do("gpsimd", "dma_start", inc=sw, n=16,
                 out=w4[:, t4 * 2048:(t4 + 1) * 2048],
                 in_=C.dram["l1w"][t4, h].rearrange("p dc f -> p (dc f)"))
        w_ready = sw.n
        raw_done = [0] * 4
        for t4 in range(4):
            for qc in range(4):
                pb = (t4 * 4 + qc) % 2
                if t4 == 0 and qc == 0:
                    C.w("tensor", sw, w_ready)
                    if h == 0:
                        C.w("tensor", sv, xnt_ready)
                C.w("tensor", sa, psA_war_a[pb])
                C.w("tensor", sv, psA_war_v[pb])
                for dc in range(16):
                    C.do("tensor", "matmul", psA[:, pb * 512:(pb + 1) * 512],
                         w4[:, t4 * 2048 + dc * 128:t4 * 2048 + (dc + 1) * 128],
                         xnT[:, dc * 2048 + qc * 512:dc * 2048 + (qc + 1) * 512],
                         start=(dc == 0), stop=(dc == 15), inc=(sm if dc == 15 else None))
                mmd = sm.n
                if t4 < 3:
                    C.w("vector", sm, mmd)
                    C.do("vector", "tensor_copy",
                         out=qraw[:, t4 * CW + CPAD + qc * 512:
                                  t4 * CW + CPAD + (qc + 1) * 512],
                         in_=psA[:, pb * 512:(pb + 1) * 512], inc=sv)
                    psA_war_v[pb] = sv.n
                    raw_done[t4] = sv.n
                else:
                    C.w("scalar", sm, mmd)
                    C.w("scalar", sm, gT_used)
                    C.do("scalar", "activation", gT[:, qc * 512:(qc + 1) * 512],
                         psA[:, pb * 512:(pb + 1) * 512], AF.Sigmoid, inc=sa)
                    psA_war_a[pb] = sa.n
                    raw_done[t4] = sa.n
        w_used = sm.n
        for t4, dstap in ((0, qkT[:, 0:2048]), (1, qkT[:, 2048:4096]), (2, vsT[:])):
            base = t4 * CW
            C.w("vector", sv, raw_done[t4])
            C.w("vector", sa, sa.n)
            C.do("vector", "tensor_scalar", ctmp[:], qraw[:, base + 13:base + 13 + 2048],
                 cons["cw"][:, t4 * 64 + h * 4:t4 * 64 + h * 4 + 1], None, ALU.mult, inc=sv)
            for s in range(1, 4):
                C.do("vector", "scalar_tensor_tensor", ctmp[:],
                     qraw[:, base + 13 + s:base + 13 + s + 2048],
                     cons["cw"][:, t4 * 64 + h * 4 + s:t4 * 64 + h * 4 + s + 1],
                     ctmp[:], ALU.mult, ALU.add, inc=sv)
            C.w("scalar", sv, sv.n)
            C.w("scalar", C.sd[t4], store_done.get((h - 1, t4), 0))
            C.do("scalar", "activation", dstap, ctmp[:], AF.Silu,
                 bias=cons["cb"][:, t4 * 16 + h:t4 * 16 + h + 1], inc=sa)
        conv_done = sa.n
        for which, (src_sb, dst_sb) in enumerate(((vsT, vtm), (gT, gat))):
            for g in range(2):
                C.w("tensor", sa, conv_done if which == 0 else raw_done[3])
                C.w("tensor", sv, ptC_war)
                for j in range(8):
                    kt = g * 8 + j
                    C.do("tensor", "transpose", ptC[:, j * 128:(j + 1) * 128],
                         src_sb[:, kt * 128:(kt + 1) * 128], cons["idn"],
                         inc=(sm if j == 7 else None))
                tdone = sm.n
                C.w("vector", sm, tdone)
                C.w("vector", C.sd[2 + which], store_done.get((h - 1, 2 + which), 0))
                C.do("vector", "tensor_copy", out=dst_sb[:, g * 1024:(g + 1) * 1024],
                     in_=ptC[:], inc=sv)
                ptC_war = sv.n
            if which == 1:
                gT_used = sm.n
        tm_done = sv.n
        C.w("sync", sa, conv_done)
        C.w("sync", sv, tm_done)
        for idx, src_sb in ((0, qkT[:, 0:2048]), (1, qkT[:, 2048:4096]),
                            (2, vtm[:]), (3, gat[:])):
            C.do("sync", "dma_start", inc=C.sd[idx], n=16, out=qkvg[idx, h], in_=src_sb)
            store_done[(h, idx)] = C.sd[idx].n
    ph.close()
    phX.close()

    C.barrier()
    emit_c_scan(C, b, colstore, lamT, lampT, bs2nT, col)
    ph0.close()

    C.barrier()
    emit_out_proj(C, b, C.dram["ovT_scr"], C.dram["l1wo"], None, x_in, x_out)


def emit_c_scan(C, b, colstore, lamT, lampT, bs2nT, col):
    nc = C.nc
    sx, sw, sm, sv, sa, sy = C.sx, C.sw, C.sm, C.sv, C.sa, C.sy
    cons = C.cons
    qkvg = C.dram["qkvg_scr"]
    ovscr = C.dram["ovT_scr"]

    ph = ExitStack()
    qT4 = C.sb(ph, "sc_q", [128, 4 * 2048], BF16)[:]
    kT4 = C.sb(ph, "sc_k", [128, 4 * 2048], BF16)[:]
    vtm4 = C.sb(ph, "sc_v", [128, 4 * 2048], BF16)[:]
    gat4 = C.sb(ph, "sc_g", [128, 4 * 2048], BF16)[:]
    ovT4 = C.sb(ph, "sc_ov", [128, 4 * 2048], BF16)[:]
    S4f = C.sb(ph, "sc_Sf", [128, 512], F32)[:]
    S4b = C.sb(ph, "sc_Sb", [128, 512], BF16)[:]
    e1m = C.sb(ph, "sc_e1", [128, 512], BF16)[:]
    e1mT = C.sb(ph, "sc_e1T", [128, 512], BF16)[:]
    e2mT = C.sb(ph, "sc_e2T", [128, 512], BF16)[:]
    ecl = C.sb(ph, "sc_ecl", [128, 512], F32)[:]
    bb4 = C.sb(ph, "sc_bb4", [128, 512], F32)[:]
    bcf = C.sb(ph, "sc_bcf", [128, 512], F32)[:]
    Xp = C.sb(ph, "sc_X", [128, 2 * 512], BF16)[:]
    Xn = C.sb(ph, "sc_Xn", [128, 2 * 512], BF16)[:]
    U4 = C.sb(ph, "sc_U", [128, 512], BF16)[:]
    RHS4 = C.sb(ph, "sc_RHS", [128, 512], F32)[:]
    MT4 = C.sb(ph, "sc_MT", [128, 512], BF16)[:]
    O4 = C.sb(ph, "sc_O", [128, 512], F32)[:]
    t4b = C.sb(ph, "sc_t4b", [128, 512], F32)[:]
    ovc = C.sb(ph, "sc_ovc", [128, 512], BF16)[:]
    Ktm4 = C.sb(ph, "sc_Ktm", [128, 512], BF16)[:]
    cst = C.sb(ph, "sc_cst", [128, 48], F32)[:]
    PA = C.ps(ph, "sc_PA", [128, 512], F32)
    PB = C.ps(ph, "sc_PB", [128, 512], F32)
    PC = C.ps(ph, "sc_PC", [128, 512], F32)
    PD = C.ps(ph, "sc_PD", [128, 512], F32)
    PE = C.ps(ph, "sc_PE", [128, 512], F32)
    PF = C.ps(ph, "sc_PF", [128, 512], F32)
    PG = C.ps(ph, "sc_PG", [128, 512], F32)
    PH = C.ps(ph, "sc_PH", [128, 1024], BF16)

    def bc4(dst, srccols):
        C.do("vector", "tensor_copy",
             out=dst.rearrange("p (a z) -> p a z", a=4),
             in_=srccols.unsqueeze(2).broadcast_to([128, 4, 128]), inc=sv)

    m3 = lambda ap: ap.rearrange("p (a z) -> p a z", a=4)
    mask3 = lambda m: m.unsqueeze(1).broadcast_to([128, 4, 128])

    ld_used = 0
    S_used = 0
    ov_stored = {}
    for hg in range(4):
        hs = [hg * 4 + i for i in range(4)]
        C.w("sync", sm, ld_used)
        for i, h in enumerate(hs):
            for t4 in range(4):
                dst = (qT4, kT4, vtm4, gat4)[t4]
                C.do("sync", "dma_start", inc=sy, n=16,
                     out=dst[:, i * 2048:(i + 1) * 2048], in_=qkvg[t4, h])
        ld_ready = sy.n
        C.w("vector", sm, S_used)
        C.do("vector", "memset", S4f[:], 0.0, inc=sv)
        C.do("vector", "memset", S4b[:], 0.0, inc=sv)
        zdone = sv.n
        war = {}
        kch = lambda i, c: kT4[:, i * 2048 + c * 128:i * 2048 + (c + 1) * 128]
        qch = lambda i, c: qT4[:, i * 2048 + c * 128:i * 2048 + (c + 1) * 128]
        for c in range(16):
            if c == 0:
                C.w("tensor", sy, ld_ready)
                C.w("tensor", sv, zdone)
            C.w("tensor", sv, war.get("PA_v", 0))
            C.w("tensor", sa, war.get("PA_a", 0))
            for i, h in enumerate(hs):
                C.do("tensor", "matmul", PA[:, i * 128:(i + 1) * 128],
                     cons["selF"][:, h * 128:(h + 1) * 128],
                     lamT[:, c * 128:(c + 1) * 128],
                     start=True, stop=True, inc=(sm if i == 3 else None))
            C.w("tensor", sv, war.get("PB_v", 0))
            for i, h in enumerate(hs):
                C.do("tensor", "matmul", PB[:, i * 128:(i + 1) * 128],
                     cons["selF"][:, h * 128:(h + 1) * 128],
                     lampT[:, c * 128:(c + 1) * 128],
                     start=True, stop=True, inc=(sm if i == 3 else None))
            C.w("tensor", sv, war.get("PC_v", 0))
            for i, h in enumerate(hs):
                C.do("tensor", "matmul", PC[:, i * 128:(i + 1) * 128],
                     cons["selF"][:, h * 128:(h + 1) * 128],
                     bs2nT[:, c * 128:(c + 1) * 128],
                     start=True, stop=True, inc=(sm if i == 3 else None))
            bc_done = sm.n
            C.w("vector", sm, bc_done)
            for i, h in enumerate(hs):
                C.do("vector", "tensor_scalar", ecl[:, i * 128:(i + 1) * 128],
                     PA[:, i * 128:(i + 1) * 128], col(c, KLAMP)[:, h:h + 1],
                     0.0, ALU.subtract, ALU.max, inc=sv)
            C.w("scalar", sv, sv.n)
            C.do("scalar", "activation", e1m[:], ecl[:], AF.Exp, scale=-1.0, inc=sa)
            C.w("vector", sa, sa.n)
            C.do("vector", "tensor_tensor", m3(e1m), m3(e1m), mask3(cons["mLTs"]),
                 ALU.mult, inc=sv, lp=True)
            for i, h in enumerate(hs):
                C.do("vector", "tensor_scalar", ecl[:, i * 128:(i + 1) * 128],
                     PB[:, i * 128:(i + 1) * 128], col(c, KLAM)[:, h:h + 1],
                     0.0, ALU.subtract, ALU.min, inc=sv)
            war["PB_v"] = sv.n
            C.w("scalar", sv, sv.n)
            C.do("scalar", "activation", e1mT[:], ecl[:], AF.Exp, inc=sa)
            C.w("vector", sa, sa.n)
            C.do("vector", "tensor_tensor", m3(e1mT), m3(e1mT), mask3(cons["mUTs"]),
                 ALU.mult, inc=sv, lp=True)
            for i, h in enumerate(hs):
                C.do("vector", "tensor_scalar", ecl[:, i * 128:(i + 1) * 128],
                     PA[:, i * 128:(i + 1) * 128], col(c, KLAM)[:, h:h + 1],
                     0.0, ALU.subtract, ALU.min, inc=sv)
            war["PA_v"] = sv.n
            C.w("scalar", sv, sv.n)
            C.do("scalar", "activation", e2mT[:], ecl[:], AF.Exp, inc=sa)
            war["PA_a"] = sa.n
            C.w("vector", sa, sa.n)
            C.do("vector", "tensor_tensor", m3(e2mT), m3(e2mT), mask3(cons["mUTis"]),
                 ALU.mult, inc=sv, lp=True)
            emat_done = sv.n
            C.w("vector", sm, bc_done)
            C.do("vector", "tensor_copy", out=bb4[:], in_=PC[:], inc=sv)
            war["PC_v"] = sv.n
            C.w("tensor", sv, war.get("PD_v", 0))
            for i in range(4):
                C.do("tensor", "matmul", PD[:, i * 128:(i + 1) * 128],
                     kch(i, c), kch(i, c), start=True, stop=True,
                     inc=(sm if i == 3 else None))
            kk_done = sm.n
            C.w("vector", sm, kk_done)
            C.w("vector", sv, emat_done)
            bc4(bcf, colstore[:, c * 128 + KBS2N * 16 + hg * 4:
                              c * 128 + KBS2N * 16 + hg * 4 + 4])
            C.do("vector", "tensor_tensor", ecl[:], PD[:], e1m[:], ALU.mult, inc=sv)
            C.do("vector", "tensor_tensor", Xp[:, 0:512], ecl[:], bcf[:], ALU.mult,
                 inc=sv, lp=True)
            C.do("vector", "tensor_tensor", ecl[:], PD[:], e1mT[:], ALU.mult, inc=sv)
            war["PD_v"] = sv.n
            C.do("vector", "tensor_tensor", Xp[:, 512:1024], ecl[:], bb4[:], ALU.mult,
                 inc=sv, lp=True)
            x_done = sv.n
            C.w("tensor", sv, war.get("PE_v", 0))
            for i in range(4):
                C.do("tensor", "matmul", PE[:, i * 128:(i + 1) * 128],
                     kch(i, c), S4b[:, i * 128:(i + 1) * 128], start=True, stop=True,
                     inc=(sm if i == 3 else None))
            ks_done = sm.n
            C.w("vector", sm, ks_done)
            for i, h in enumerate(hs):
                C.do("vector", "scalar_tensor_tensor", RHS4[:, i * 128:(i + 1) * 128],
                     PE[:, i * 128:(i + 1) * 128], col(c, KNGPS)[:, h:h + 1],
                     vtm4[:, i * 2048 + c * 128:i * 2048 + (c + 1) * 128],
                     ALU.mult, ALU.add, inc=sv)
            for i, h in enumerate(hs):
                C.do("vector", "tensor_scalar", U4[:, i * 128:(i + 1) * 128],
                     RHS4[:, i * 128:(i + 1) * 128], col(c, KBETA)[:, h:h + 1], None,
                     ALU.mult, inc=sv, lp=True)
            war["PE_v"] = sv.n
            u_init = sv.n
            C.w("tensor", sv, x_done)
            C.w("tensor", sv, u_init)
            xcur, xnxt = Xp, Xn
            pf_v = war.get("PF_v", 0)
            pg_v = war.get("PG_v", 0)
            for stage in range(7):
                C.w("tensor", sv, pf_v)
                for i in range(4):
                    C.do("tensor", "matmul", PF[:, i * 128:(i + 1) * 128],
                         xcur[:, 512 + i * 128:512 + (i + 1) * 128],
                         U4[:, i * 128:(i + 1) * 128], start=True, stop=True,
                         inc=(sm if i == 3 else None))
                C.w("vector", sm, sm.n)
                C.do("vector", "tensor_tensor", U4[:], U4[:], PF[:], ALU.add,
                     inc=sv, lp=True)
                pf_v = sv.n
                if stage < 6:
                    C.w("tensor", sv, pg_v)
                    for i in range(4):
                        C.do("tensor", "matmul", PG[:, i * 128:(i + 1) * 128],
                             xcur[:, 512 + i * 128:512 + (i + 1) * 128],
                             xcur[:, i * 128:(i + 1) * 128], start=True, stop=True,
                             inc=(sm if i == 3 else None))
                    C.w("vector", sm, sm.n)
                    C.do("vector", "tensor_copy", out=xnxt[:, 0:512], in_=PG[:], inc=sv)
                    x2_evac = sv.n
                    C.w("tensor", sv, x2_evac)
                    for i in range(4):
                        C.do("tensor", "matmul", PG[:, i * 128:(i + 1) * 128],
                             xcur[:, i * 128:(i + 1) * 128],
                             xcur[:, 512 + i * 128:512 + (i + 1) * 128], start=True,
                             stop=True, inc=(sm if i == 3 else None))
                    C.w("vector", sm, sm.n)
                    C.do("vector", "tensor_copy", out=xnxt[:, 512:1024], in_=PG[:],
                         inc=sv)
                    pg_v = sv.n
                    C.w("tensor", sv, pg_v)
                    xcur, xnxt = xnxt, xcur
            war["PF_v"] = pf_v
            war["PG_v"] = pg_v
            u_done = sv.n
            C.w("tensor", sv, u_done)
            C.w("tensor", sv, war.get("PG2_v", 0))
            for i in range(4):
                C.do("tensor", "matmul", PG[:, i * 128:(i + 1) * 128],
                     kch(i, c), qch(i, c), start=True, stop=True,
                     inc=(sm if i == 3 else None))
            kq_done = sm.n
            C.w("vector", sm, kq_done)
            C.do("vector", "tensor_tensor", MT4[:], PG[:], e2mT[:], ALU.mult,
                 inc=sv, lp=True)
            mt_done = sv.n
            C.w("tensor", sv, war.get("PE_v", 0))
            for i in range(4):
                C.do("tensor", "matmul", PE[:, i * 128:(i + 1) * 128],
                     qch(i, c), S4b[:, i * 128:(i + 1) * 128], start=True, stop=True,
                     inc=(sm if i == 3 else None))
            qs_done = sm.n
            C.w("tensor", sv, mt_done)
            for i in range(4):
                C.do("tensor", "matmul", PG[:, i * 128:(i + 1) * 128],
                     MT4[:, i * 128:(i + 1) * 128], U4[:, i * 128:(i + 1) * 128],
                     start=True, stop=True, inc=(sm if i == 3 else None))
            mu_done = sm.n
            C.w("vector", sm, mu_done)
            C.do("vector", "tensor_copy", out=t4b[:], in_=PG[:], inc=sv)
            war["PG2_v"] = sv.n
            C.w("vector", sm, qs_done)
            for i, h in enumerate(hs):
                C.do("vector", "scalar_tensor_tensor", O4[:, i * 128:(i + 1) * 128],
                     PE[:, i * 128:(i + 1) * 128], col(c, KGT)[:, h:h + 1],
                     t4b[:, i * 128:(i + 1) * 128], ALU.mult, ALU.add, inc=sv)
            war["PE_v"] = sv.n
            for i in range(4):
                C.do("vector", "reduce_sum", cst[:, i:i + 1],
                     O4[:, i * 128:(i + 1) * 128], axis=AXX, inc=sv)
            C.w("scalar", sv, sv.n)
            C.do("scalar", "activation", ecl[:], O4[:], AF.Square, inc=sa)
            C.w("vector", sa, sa.n)
            for i in range(4):
                C.do("vector", "reduce_sum", cst[:, 4 + i:5 + i],
                     ecl[:, i * 128:(i + 1) * 128], axis=AXX, inc=sv)
            C.do("vector", "tensor_scalar", cst[:, 8:12], cst[:, 0:4], 1.0 / 128, None,
                 ALU.mult, inc=sv)
            C.do("vector", "tensor_tensor", cst[:, 12:16], cst[:, 8:12], cst[:, 8:12],
                 ALU.mult, inc=sv)
            C.do("vector", "tensor_scalar", cst[:, 28:32], cst[:, 4:8], 1.0 / 128,
                 None, ALU.mult, inc=sv)
            C.do("vector", "tensor_sub", cst[:, 16:20], cst[:, 28:32], cst[:, 12:16],
                 inc=sv)
            C.w("scalar", sv, sv.n)
            C.do("scalar", "activation", cst[:, 20:24], cst[:, 16:20], AF.Sqrt,
                 bias=cons["fcol"][:, FCOL["eps"]:FCOL["eps"] + 1], inc=sa)
            C.w("vector", sa, sa.n)
            C.do("vector", "reciprocal", cst[:, 24:28], cst[:, 20:24], inc=sv)
            mu_bc = cst[:, 8:12].unsqueeze(2).broadcast_to([128, 4, 128])
            rstd_bc = cst[:, 24:28].unsqueeze(2).broadcast_to([128, 4, 128])
            gview = gat4[:].rearrange("p (a t) -> p a t", a=4)[:, :, c * 128:(c + 1) * 128]
            C.do("vector", "tensor_tensor", m3(O4), m3(O4), mu_bc, ALU.subtract, inc=sv)
            C.do("vector", "tensor_tensor", m3(O4), m3(O4), rstd_bc, ALU.mult, inc=sv)
            C.do("vector", "tensor_tensor", m3(O4), m3(O4), mask3(cons["lnwbc"]),
                 ALU.mult, inc=sv)
            C.do("vector", "tensor_tensor", m3(O4), m3(O4), mask3(cons["lnbbc"]),
                 ALU.add, inc=sv)
            C.w("vector", sm, war.get("ovc_m", 0))
            C.do("vector", "tensor_tensor", m3(ovc), m3(O4), gview, ALU.mult,
                 inc=sv, lp=True)
            ov_ready = sv.n
            C.w("tensor", sv, ov_ready)
            C.w("tensor", sv, war.get("PH_v", 0))
            for i in range(4):
                C.do("tensor", "transpose", PH[:, i * 128:(i + 1) * 128],
                     ovc[:, i * 128:(i + 1) * 128], cons["idn"],
                     inc=(sm if i == 3 else None))
            war["ovc_m"] = sm.n
            for i in range(4):
                C.do("tensor", "transpose", PH[:, 512 + i * 128:512 + (i + 1) * 128],
                     kch(i, c), cons["idn"], inc=(sm if i == 3 else None))
            tp_done = sm.n
            C.w("vector", sm, tp_done)
            ovT_dst = ovT4[:].rearrange("p (a t) -> p a t", a=4)[:, :, c * 128:(c + 1) * 128]
            C.w("vector", sx, ov_stored.get(hg - 1, 0))
            C.do("vector", "tensor_copy", out=ovT_dst,
                 in_=PH[:, 0:512].rearrange("p (a z) -> p a z", a=4), inc=sv)
            C.do("vector", "tensor_copy", out=cst[:, 32:36],
                 in_=e2mT[:, 127:512:128], inc=sv)
            for i in range(4):
                C.do("vector", "tensor_scalar", Ktm4[:, i * 128:(i + 1) * 128],
                     PH[:, 512 + i * 128:512 + (i + 1) * 128],
                     cst[:, 32 + i:33 + i], None, ALU.mult,
                     inc=sv, lp=True)
            war["PH_v"] = sv.n
            ktm_done = sv.n
            C.w("tensor", sv, ktm_done)
            C.w("tensor", sv, war.get("PD2_v", 0))
            for i in range(4):
                C.do("tensor", "matmul", PD[:, i * 128:(i + 1) * 128],
                     Ktm4[:, i * 128:(i + 1) * 128], U4[:, i * 128:(i + 1) * 128],
                     start=True, stop=True, inc=(sm if i == 3 else None))
            su_done = sm.n
            C.w("vector", sm, su_done)
            bc4(bcf, col(c, KGC)[:, hg * 4:hg * 4 + 4])
            C.do("vector", "tensor_tensor", t4b[:], S4f[:], bcf[:], ALU.mult, inc=sv)
            C.do("vector", "tensor_tensor", S4f[:], t4b[:], PD[:], ALU.add, inc=sv)
            war["PD2_v"] = sv.n
            C.do("vector", "tensor_copy", out=S4b[:], in_=S4f[:], inc=sv, lp=True)
            war["PD_v"] = sv.n
        ov_done = sv.n
        C.w("sync", sv, ov_done)
        for i, h in enumerate(hs):
            C.do("sync", "dma_start", inc=sx, n=16, out=ovscr[b, h],
                 in_=ovT4[:, i * 2048:(i + 1) * 2048])
        ov_stored[hg] = sx.n
        ld_used = sm.n
        S_used = sm.n
    ph.close()


def emit_block_ffn(C, b, x_in, x_out, kind):
    nc = C.nc
    sx, sw, sm, sv, sa, sy = C.sx, C.sw, C.sm, C.sv, C.sa, C.sy
    isb = kind == "B"
    pre = "f0_" if isb else "f1_"
    wg_d, wu_d, wd_d = C.dram[pre + "wg"], C.dram[pre + "wu"], C.dram[pre + "wd"]

    ph0 = ExitStack()
    xnT = C.sb(ph0, "xnTf", [128, 16 * 2048], BF16)[:]
    C.barrier()
    xnt_ready = emit_ln_to_xnT(C, x_in[b], xnT)

    C.barrier()
    ph = ExitStack()
    wgsl = C.sb(ph, "f1wg", [128, 2 * 2048], BF16)
    wusl = C.sb(ph, "f1wu", [128, 2 * 2048], BF16)
    gs = C.sb(ph, "f1g", [128, 2 * 2048], F32)
    us = C.sb(ph, "f1u", [128, 2 * 2048], F32)
    hs = C.sb(ph, "f1h", [128, 2 * 2048], BF16)
    psg = C.ps(ph, "f1psg", [128, 2 * 512], F32)
    psu = C.ps(ph, "f1psu", [128, 2 * 512], F32)
    h_scr = C.dram["h_scr"]
    w_used = 0
    psg_war = [0, 0]
    psu_war = [0, 0]
    h_stored = 0
    h_prev_done = 0
    for sl in range(32):
        if DBG_BAR_F1:
            C.barrier()
        C.w("gpsimd", sm, w_used)
        for m in range(2):
            gm = sl * 2 + m
            C.do("gpsimd", "dma_start", inc=sw, n=16,
                 out=wgsl[:, m * 2048:(m + 1) * 2048],
                 in_=wg_d[gm].rearrange("p dc f -> p (dc f)"))
            C.do("gpsimd", "dma_start", inc=sw, n=16,
                 out=wusl[:, m * 2048:(m + 1) * 2048],
                 in_=wu_d[gm].rearrange("p dc f -> p (dc f)"))
        w_ready = sw.n
        w_fixed = 0
        if isb:
            C.w("vector", sw, w_ready)
            C.do("vector", "tensor_scalar_add", wgsl[:], wgsl[:], -128.0, inc=sv)
            C.do("vector", "tensor_scalar_add", wusl[:], wusl[:], -128.0, inc=sv)
            w_fixed = sv.n
        for m in range(2):
            for qc in range(4):
                pb = qc % 2
                if m == 0 and qc == 0:
                    if isb:
                        C.w("tensor", sv, w_fixed)
                    else:
                        C.w("tensor", sw, w_ready)
                    if sl == 0:
                        C.w("tensor", sv, xnt_ready)
                C.w("tensor", sa, psg_war[pb])
                for dc in range(16):
                    C.do("tensor", "matmul", psg[:, pb * 512:(pb + 1) * 512],
                         wgsl[:, m * 2048 + dc * 128:m * 2048 + (dc + 1) * 128],
                         xnT[:, dc * 2048 + qc * 512:dc * 2048 + (qc + 1) * 512],
                         start=(dc == 0), stop=(dc == 15), inc=(sm if dc == 15 else None))
                g_done = sm.n
                C.w("tensor", sv, psu_war[pb])
                for dc in range(16):
                    C.do("tensor", "matmul", psu[:, pb * 512:(pb + 1) * 512],
                         wusl[:, m * 2048 + dc * 128:m * 2048 + (dc + 1) * 128],
                         xnT[:, dc * 2048 + qc * 512:dc * 2048 + (qc + 1) * 512],
                         start=(dc == 0), stop=(dc == 15), inc=(sm if dc == 15 else None))
                u_done = sm.n
                col = m * 2048 + qc * 512
                C.w("scalar", sm, g_done)
                C.w("scalar", sv, h_prev_done)
                if isb:
                    C.do("scalar", "activation", gs[:, col:col + 512],
                         psg[:, pb * 512:(pb + 1) * 512], AF.Silu,
                         scale=C.cons[pre + "gsc"][:, sl * 2 + m:sl * 2 + m + 1], inc=sa)
                else:
                    C.do("scalar", "activation", gs[:, col:col + 512],
                         psg[:, pb * 512:(pb + 1) * 512], AF.Silu, inc=sa)
                psg_war[pb] = sa.n
                C.w("vector", sm, u_done)
                if isb:
                    C.do("vector", "tensor_scalar",
                         us[:, col:col + 512], psu[:, pb * 512:(pb + 1) * 512],
                         C.cons[pre + "usc"][:, sl * 2 + m:sl * 2 + m + 1], None,
                         ALU.mult, inc=sv)
                else:
                    C.do("vector", "tensor_copy", out=us[:, col:col + 512],
                         in_=psu[:, pb * 512:(pb + 1) * 512], inc=sv)
                psu_war[pb] = sv.n
                C.w("vector", sa, psg_war[pb])
                if m == 0 and qc == 0:
                    C.w("vector", sx, h_stored)
                C.do("vector", "tensor_tensor", hs[:, col:col + 512],
                     gs[:, col:col + 512], us[:, col:col + 512], ALU.mult, inc=sv)
        h_done = sv.n
        h_prev_done = h_done
        C.w("sync", sv, h_done)
        dst = h_scr[sl * 256:(sl + 1) * 256, :].rearrange("(mt p) t -> p mt t", p=128)
        C.do("sync", "dma_start", inc=sx, n=16,
             out=dst, in_=hs[:].rearrange("p (mt t) -> p mt t", mt=2))
        h_stored = sx.n
        w_used = sm.n
    ph.close()
    ph0.close()

    C.barrier()
    ph = ExitStack()
    if isb:
        build_bc_tile(C, ph, "f0_dscbc", "f0_dsc", 2048)
    h2 = C.sb(ph, "f2h", [128, 64 * 1024], BF16)
    ybf = C.sb(ph, "f2y", [128, 8 * 2048], BF16)
    wd2 = C.sb(ph, "f2wd", [128, 2 * 512], BF16)
    psY = C.ps(ph, "f2psY", [128, 8 * 512], F32)
    C.bufs["xt3"] = C.sb(ph, "f2xt3", [128, 2 * D], F32)[:]
    C.bufs["sqs2"] = C.sb(ph, "f2sqs2", [128, D], BF16)[:]
    h2_used = 0
    yevac = [0] * 8
    wd_used = [0, 0]
    for tc2 in range(2):
        C.w("sync", sm, h2_used)
        for ic in range(64):
            C.do("sync", "dma_start", inc=sy, n=16,
                 out=h2[:, ic * 1024:(ic + 1) * 1024],
                 in_=h_scr[ic * 128:(ic + 1) * 128, tc2 * 1024:(tc2 + 1) * 1024])
        h2_ready = sy.n
        for hc in range(4):
            if DBG_BAR_F2:
                C.barrier()
            for ic in range(64):
                wb = ic % 2
                C.w("gpsimd", sm, wd_used[wb])
                C.do("gpsimd", "dma_start", inc=C.sd[wb], n=16,
                     out=wd2[:, wb * 512:(wb + 1) * 512],
                     in_=wd_d[ic][:, hc * 512:(hc + 1) * 512])
                wd_ready = C.sd[wb].n
                wfix = 0
                if isb:
                    C.w("vector", C.sd[wb], wd_ready)
                    C.w("vector", sm, wd_used[wb])
                    C.do("vector", "tensor_scalar_add", wd2[:, wb * 512:(wb + 1) * 512],
                         wd2[:, wb * 512:(wb + 1) * 512], -128.0, inc=sv)
                    wfix = sv.n
                if ic == 0:
                    C.w("tensor", sy, h2_ready)
                    for j in range(8):
                        C.w("tensor", sv, yevac[j])
                if isb:
                    C.w("tensor", sv, wfix)
                else:
                    C.w("tensor", C.sd[wb], wd_ready)
                for j in range(8):
                    C.do("tensor", "matmul", psY[:, j * 512:(j + 1) * 512],
                         h2[:, ic * 1024 + j * 128:ic * 1024 + (j + 1) * 128],
                         wd2[:, wb * 512:(wb + 1) * 512],
                         start=(ic == 0), stop=(ic == 63),
                         inc=(sm if j == 7 else None))
                wd_used[wb] = sm.n
            sweep_done = sm.n
            C.w("vector", sm, sweep_done)
            if hc == 0:
                C.w("vector", sa, C.war.get("res_sq", 0))
            for j in range(8):
                dsty = ybf[:, j * 2048 + hc * 512:j * 2048 + (hc + 1) * 512]
                if isb:
                    C.do("vector", "tensor_tensor", dsty, psY[:, j * 512:(j + 1) * 512],
                         C.cons["f0_dscbc"][:, hc * 512:(hc + 1) * 512], ALU.mult, inc=sv)
                else:
                    C.do("vector", "tensor_copy", out=dsty,
                         in_=psY[:, j * 512:(j + 1) * 512], inc=sv)
                yevac[j] = sv.n
        y_ready = sv.n
        if "dbg_ybf" in C.dram:
            C.w("sync", sv, y_ready)
            C.do("sync", "dma_start", inc=sx, n=16,
                 out=C.dram["dbg_ybf"][b, tc2], in_=ybf[:])
        for j in range(8):
            i = tc2 * 8 + j
            emit_post_ln_residual(C, ybf[:, j * 2048:(j + 1) * 2048],
                                  x_in[b][i * 128:(i + 1) * 128, :],
                                  x_out[b][i * 128:(i + 1) * 128, :],
                                  j % 2, y_ready)
        h2_used = sm.n
    ph.close()


def build_program(blocks):
    nc = bass.Bass("TRN2", target_bir_lowering=False)
    es = ExitStack()
    C = Ctx(nc, es)
    sx, sw, sm, sv, sa = C.sx, C.sw, C.sm, C.sv, C.sa

    C.dram["x_in"] = nc.dram_tensor("x_in", [NB, T, D], F32, kind="ExternalInput").ap()
    C.dram["x_out"] = nc.dram_tensor("x_out", [NB, T, D], F32, kind="ExternalOutput").ap()

    def din(name, shape, dt):
        C.dram[name] = nc.dram_tensor(name, shape, dt, kind="ExternalInput").ap()
        return C.dram[name]

    din("c_idn", [128, 128], BF16)
    din("c_onesc", [128, 1], BF16)
    din("c_onesr", [1, 128], BF16)
    din("c_onesrf", [1, 128], F32)
    din("c_fcol", [128, 8], F32)
    if "A" in blocks:
        din("wqkv", [3, H, 128, 16, 128], F32)
        din("bqkv", [128, 48], F32)
        din("wout", [16, 128, 2048], F32)
        din("bout", [1, 2048], F32)
        din("c_ropeC", [64, 2048], BF16)
        din("c_ropeS", [64, 2048], BF16)
        C.dram["attnT"] = nc.dram_tensor("attnT", [NB, H, 128, 2048], BF16,
                                         kind="Internal").ap()
    if "B" in blocks or "D" in blocks:
        C.dram["h_scr"] = nc.dram_tensor("h_scr", [FI, T], BF16, kind="Internal").ap()
    if "B" in blocks:
        din("f0_wg", [64, 128, 16, 128], I32)
        din("f0_wu", [64, 128, 16, 128], I32)
        din("f0_wd", [64, 128, 2048], I32)
        din("c_f0_gsc", [128, 64], F32)
        din("c_f0_usc", [128, 64], F32)
        din("f0_dsc", [1, 2048], F32)
    if "D" in blocks:
        din("f1_wg", [64, 128, 16, 128], F32)
        din("f1_wu", [64, 128, 16, 128], F32)
        din("f1_wd", [64, 128, 2048], F32)
    if "C" in blocks:
        din("l1w", [4, H, 128, 16, 128], F32)
        din("l1wo", [16, 128, 2048], F32)
        din("c_wab", [128, 512], F32)
        din("c_bab", [16, 2], F32)
        din("c_cw", [128, 192], F32)
        din("c_cb", [128, 48], F32)
        din("c_selF", [16, 2048], F32)
        din("c_idnf", [128, 128], F32)
        din("c_mLTs", [128, 128], BF16)
        din("c_mUTs", [128, 128], BF16)
        din("c_mUTis", [128, 128], BF16)
        din("c_lnw", [1, 128], F32)
        din("c_lnb", [1, 128], F32)
        C.dram["qkvg_scr"] = nc.dram_tensor("qkvg_scr", [4, H, 128, 2048], BF16,
                                            kind="Internal").ap()
        C.dram["ovT_scr"] = nc.dram_tensor("ovT_scr", [NB, H, 128, 2048], BF16,
                                           kind="Internal").ap()

    B = C.bufs
    B["cols"] = es.enter_context(nc.sbuf_tensor("cols", [128, 16], F32))[:]
    B["cols2"] = es.enter_context(nc.sbuf_tensor("cols2", [128, 16], F32))[:]

    sc = C.sc
    for nm in ("idn", "onesc", "onesr"):
        C.cons[nm] = es.enter_context(
            nc.sbuf_tensor("sb_" + nm, list(C.dram["c_" + nm].shape), BF16))[:]
        C.do("sync", "dma_start", inc=sc, n=16, out=C.cons[nm], in_=C.dram["c_" + nm])
    C.cons["fcol"] = es.enter_context(nc.sbuf_tensor("sb_fcol", [128, 8], F32))[:]
    C.do("sync", "dma_start", inc=sc, n=16, out=C.cons["fcol"], in_=C.dram["c_fcol"])
    C.cons["onesrf"] = es.enter_context(nc.sbuf_tensor("sb_onesrf", [1, 128], F32))[:]
    C.do("sync", "dma_start", inc=sc, n=16, out=C.cons["onesrf"], in_=C.dram["c_onesrf"])
    if "B" in blocks:
        for nm in ("f0_gsc", "f0_usc"):
            C.cons[nm] = es.enter_context(nc.sbuf_tensor("sb_" + nm, [128, 64], F32))[:]
            C.do("sync", "dma_start", inc=sc, n=16, out=C.cons[nm], in_=C.dram["c_" + nm])
    if "C" in blocks:
        for nm, shp, dt_ in (("bab", [16, 2], F32),
                             ("cw", [128, 192], F32), ("cb", [128, 48], F32),
                             ("idnf", [128, 128], F32),
                             ("mLTs", [128, 128], BF16), ("mUTs", [128, 128], BF16),
                             ("mUTis", [128, 128], BF16)):
            C.cons[nm] = es.enter_context(nc.sbuf_tensor("sb_" + nm, shp, dt_))[:]
            C.do("sync", "dma_start", inc=sc, n=16, out=C.cons[nm], in_=C.dram["c_" + nm])
        C.cons["wab"] = es.enter_context(nc.sbuf_tensor("sb_wab", [128, 512], BF16))[:]
        C.do("gpsimd", "dma_start", inc=sc, n=16, out=C.cons["wab"], in_=C.dram["c_wab"])

    if "A" in blocks:
        C.cons["bqkv"] = es.enter_context(nc.sbuf_tensor("sb_bqkv", [128, 48], F32))[:]
        C.do("sync", "dma_start", inc=sc, n=16, out=C.cons["bqkv"], in_=C.dram["bqkv"])
    C.war["consts_sx"] = 0
    for _e in ("tensor", "vector", "scalar", "gpsimd", "sync"):
        C.w(_e, sc, sc.n)



    seq = [blk for blk in "ABCD" if blk in blocks]
    io = [C.dram["x_in"]]
    for i in range(len(seq) - 1):
        io.append(nc.dram_tensor("xmid%d" % i, [NB, T, D], F32, kind="Internal").ap())
    io.append(C.dram["x_out"])
    for idx, blk in enumerate(seq):
        src_ap, dst_ap = io[idx], io[idx + 1]
        for b in range(NB):
            if blk == "A":
                emit_block_a(C, b, src_ap, dst_ap)
            elif blk in ("B", "D"):
                emit_block_ffn(C, b, src_ap, dst_ap, blk)
            elif blk == "C":
                emit_block_c(C, b, src_ap, dst_ap)
            else:
                raise NotImplementedError(blk)

    for s_ in C.sems.values():
        C.w("sync", s_, s_.n)
    C.play()
    es.close()
    return nc


def _consts_a():
    inv = (1.0 / 10000.0 ** (np.arange(0, 64, 2, dtype=np.float32) / 64)).astype(np.float32)
    t = np.arange(T, dtype=np.float32)
    freqs = np.einsum("i,j->ij", t, inv).astype(np.float32)
    cos = np.cos(freqs).astype(np.float32)
    sin = np.sin(freqs).astype(np.float32)
    ropeC = np.ascontiguousarray(np.concatenate([cos, cos], axis=1).T)
    ropeS = np.ascontiguousarray(np.concatenate([-sin, sin], axis=1).T)
    return ropeC.astype(BFNP), ropeS.astype(BFNP)


def prep_shared(inputs, blocks):
    d = {}
    d["c_idn"] = np.eye(128, dtype=BFNP)
    fc = np.zeros((128, 8), dtype=np.float32)
    for k, j in FCOL.items():
        fc[:, j] = FCOL_VALS[k]
    d["c_fcol"] = fc
    d["c_onesc"] = np.ones((128, 1), dtype=BFNP)
    d["c_onesr"] = np.ones((1, 128), dtype=BFNP)
    d["c_onesrf"] = np.ones((1, 128), dtype=np.float32)
    if "A" in blocks:
        w = np.ascontiguousarray(inputs["L0_attn_w_qkv"], dtype=np.float32)
        d["wqkv"] = np.ascontiguousarray(
            w.reshape(16, 128, 3, 16, 128).transpose(2, 3, 1, 0, 4))
        d["bqkv"] = np.ascontiguousarray(
            np.asarray(inputs["L0_attn_b_qkv"], dtype=np.float32)
            .reshape(3, 16, 128).transpose(2, 1, 0).reshape(128, 48))
        d["wout"] = np.ascontiguousarray(
            np.asarray(inputs["L0_attn_w_out"], dtype=np.float32)
            .reshape(16, 128, 16, 128).transpose(2, 1, 0, 3).reshape(16, 128, 2048))
        d["bout"] = np.ascontiguousarray(
            np.asarray(inputs["L0_attn_b_out"], dtype=np.float32).reshape(1, 2048))
        rc, rs = _consts_a()
        d["c_ropeC"] = rc
        d["c_ropeS"] = rs

    def _w_in(w):
        return np.ascontiguousarray(w.T.reshape(16, 128, 64, 128).transpose(2, 1, 0, 3))

    def _w_dn(w):
        return np.ascontiguousarray(np.ascontiguousarray(w).T.reshape(64, 128, 2048))

    def _sc(s):
        return np.ascontiguousarray(np.asarray(s, np.float32).reshape(64, 128).T)

    if "B" in blocks:
        d["f0_wg"] = _w_in(np.asarray(inputs["L0_ffn_gate_w"], np.int32))
        d["f0_wu"] = _w_in(np.asarray(inputs["L0_ffn_up_w"], np.int32))
        d["f0_wd"] = _w_dn(np.asarray(inputs["L0_ffn_down_w"], np.int32))
        d["c_f0_gsc"] = _sc(inputs["L0_ffn_gate_scale"])
        d["c_f0_usc"] = _sc(inputs["L0_ffn_up_scale"])
        d["f0_dsc"] = np.ascontiguousarray(
            np.asarray(inputs["L0_ffn_down_scale"], np.float32).reshape(1, 2048))
    if "C" in blocks:
        def _wh(w):
            return np.asarray(w, np.float32).reshape(16, 128, 16, 128).transpose(2, 1, 0, 3)
        d["l1w"] = np.ascontiguousarray(np.stack([
            _wh(inputs["L1_attn_Wq"]), _wh(inputs["L1_attn_Wk"]),
            _wh(inputs["L1_attn_Wv"]), _wh(inputs["L1_attn_Wg"])]))
        d["l1wo"] = np.ascontiguousarray(
            np.asarray(inputs["L1_attn_Wo"], np.float32)
            .reshape(16, 128, 16, 128).transpose(2, 1, 0, 3).reshape(16, 128, 2048))
        wa = np.asarray(inputs["L1_attn_Wa"], np.float32).reshape(16, 128, 16)
        wb = np.asarray(inputs["L1_attn_Wb"], np.float32).reshape(16, 128, 16)
        d["c_wab"] = np.ascontiguousarray(np.concatenate(
            [wa.transpose(1, 0, 2).reshape(128, 256),
             wb.transpose(1, 0, 2).reshape(128, 256)], axis=1))
        d["c_bab"] = np.ascontiguousarray(np.stack(
            [np.asarray(inputs["L1_attn_ba"], np.float32),
             np.asarray(inputs["L1_attn_bb"], np.float32)], axis=1))
        conv_ws = (inputs["L1_attn_conv_q_w"], inputs["L1_attn_conv_k_w"],
                   inputs["L1_attn_conv_v_w"])
        conv_bs = (inputs["L1_attn_conv_q_b"], inputs["L1_attn_conv_k_b"],
                   inputs["L1_attn_conv_v_b"])
        d["c_cw"] = np.ascontiguousarray(np.concatenate(
            [np.asarray(w_, np.float32).reshape(16, 128, 4)
             .transpose(1, 0, 2).reshape(128, 64) for w_ in conv_ws], axis=1))
        d["c_cb"] = np.ascontiguousarray(np.concatenate(
            [np.asarray(b_, np.float32).reshape(16, 128).T for b_ in conv_bs], axis=1))
        sel = np.zeros((16, 2048), np.float32)
        for hh in range(16):
            sel[hh, hh * 128:(hh + 1) * 128] = 1.0
        d["c_selF"] = sel
        d["c_idnf"] = np.eye(128, dtype=np.float32)
        ii = np.arange(128)
        d["c_mLTs"] = (ii[None, :] < ii[:, None]).astype(BFNP)
        d["c_mUTs"] = (ii[None, :] > ii[:, None]).astype(BFNP)
        d["c_mUTis"] = ((ii[None, :] >= ii[:, None]).astype(np.float32)
                        * np.float32(128.0 ** -0.5)).astype(BFNP)
        d["c_lnw"] = np.asarray(inputs["L1_attn_ln_w"], np.float32).reshape(1, 128)
        d["c_lnb"] = np.asarray(inputs["L1_attn_ln_b"], np.float32).reshape(1, 128)
    if "D" in blocks:
        d["f1_wg"] = _w_in(np.asarray(inputs["L1_ffn_w_gate"], np.float32))
        d["f1_wu"] = _w_in(np.asarray(inputs["L1_ffn_w_up"], np.float32))
        d["f1_wd"] = _w_dn(np.asarray(inputs["L1_ffn_w_down"], np.float32))
    return d


def run_blocks(x_full, inputs, blocks):
    shared = prep_shared(inputs, blocks)
    nc = build_program(blocks)
    in_maps = []
    for c in range(NCORES):
        m = dict(shared)
        m["x_in"] = np.ascontiguousarray(x_full[c * NB:(c + 1) * NB], dtype=np.float32)
        in_maps.append(m)
    res = run_bass_kernel_spmd(nc, in_maps, core_ids=list(range(NCORES)))
    return np.concatenate([r["x_out"] for r in res.results], axis=0)


def kernel(**inputs):
    x = np.ascontiguousarray(np.asarray(inputs["L0_attn_x"], dtype=np.float32))
    return run_blocks(x, inputs, ("A", "B", "C", "D"))


# version 07
# speedup vs baseline: 1.1777x; 1.1777x over previous
import numpy as np
import ml_dtypes
from contextlib import ExitStack
import concourse.bass as bass
import concourse.mybir as mybir
from concourse.bass_utils import run_bass_kernel_spmd

F32 = mybir.dt.float32
BF16 = mybir.dt.bfloat16
I32 = mybir.dt.int32
AF = mybir.ActivationFunctionType
ALU = mybir.AluOpType
AXX = mybir.AxisListType.X
BFNP = ml_dtypes.bfloat16

NCORES = 8
NB = 2
T = 2048
D = 2048
NT = T // 128
H = 16
DH = 128
EPS = 1e-5
A0_SCALE = DH ** -0.5
A0_SHIFT = 16.0
FI = 8192
FCOL = {"eps": 0, "nshift": 1, "lns1": 2, "zero": 3}
FCOL_VALS = {"eps": EPS, "nshift": -A0_SHIFT, "lns1": float(np.log(128.0 ** -0.5)), "zero": 0.0}
DBG_BAR_F1 = False
DBG_BAR_F2 = False


class Sem:
    __slots__ = ("h", "n")

    def __init__(self, h):
        self.h = h
        self.n = 0


class Ctx:
    ENG = ("tensor", "vector", "scalar", "gpsimd", "sync")

    def __init__(self, nc, es):
        self.nc = nc
        self.es = es
        self.q = {e: [] for e in self.ENG}
        self.trace = {e: [] for e in self.ENG}
        self.sems = {}
        for k in ("x", "w", "m", "v", "a", "c", "y", "d0", "d1", "d2", "d3"):
            self.sems[k] = Sem(es.enter_context(nc.semaphore("s_" + k)))
        self.sx = self.sems["x"]
        self.sw = self.sems["w"]
        self.sm = self.sems["m"]
        self.sv = self.sems["v"]
        self.sa = self.sems["a"]
        self.sc = self.sems["c"]
        self.sy = self.sems["y"]
        self.sd = [self.sems["d%d" % i] for i in range(4)]
        self._semkey = {id(s): k for k, s in self.sems.items()}
        self.dram = {}
        self.cons = {}
        self.bufs = {}
        self.war = {}
        self._u = 0

    def sb(self, stack, name, shape, dt):
        self._u += 1
        return stack.enter_context(self.nc.sbuf_tensor(f"{name}_{self._u}", shape, dt))

    def ps(self, stack, name, shape, dt):
        self._u += 1
        return stack.enter_context(self.nc.psum_tensor(f"{name}_{self._u}", shape, dt))

    def w(self, eng, sem, val):
        if val > 0:
            self.q[eng].append(lambda e, s=sem.h, v=val: e.wait_ge(s, v))
            self.trace[eng].append(("w", self._semkey[id(sem)], val))

    SELF_SYNC = {"vector": "v", "scalar": "a", "gpsimd": "w"}

    def do(self, eng, meth, *a, inc=None, n=1, lp=False, **kw):
        nc = self.nc
        if eng in self.SELF_SYNC and meth != "dma_start":
            own = self.sems[self.SELF_SYNC[eng]]
            self.w(eng, own, own.n)
            if inc is None:
                inc = own
        if inc is not None:
            inc.n += n

            def f(e, m=meth, a=a, kw=kw, s=inc.h, nn=n, lp=lp):
                if lp:
                    with nc.allow_low_precision("bf16 path by design"):
                        getattr(e, m)(*a, **kw).then_inc(s, nn)
                else:
                    getattr(e, m)(*a, **kw).then_inc(s, nn)
            f.meta = meth
            self.trace[eng].append(("i", self._semkey[id(inc)], n))
        else:
            def f(e, m=meth, a=a, kw=kw, lp=lp):
                if lp:
                    with nc.allow_low_precision("bf16 path by design"):
                        getattr(e, m)(*a, **kw)
                else:
                    getattr(e, m)(*a, **kw)
            f.meta = meth
            self.trace[eng].append(("i", None, 0))
        self.q[eng].append(f)
        return inc.n if inc is not None else None

    def barrier(self):
        for e in self.ENG:
            for s in self.sems.values():
                self.w(e, s, s.n)

    def check(self):
        pos = {e: 0 for e in self.ENG}
        cnt = {k: 0 for k in self.sems}
        total = sum(len(t) for t in self.trace.values())
        done = 0
        progress = True
        while progress:
            progress = False
            for e in self.ENG:
                t = self.trace[e]
                while pos[e] < len(t):
                    kind, key, val = t[pos[e]]
                    if kind == "w" and cnt[key] < val:
                        break
                    if kind == "i" and key is not None:
                        cnt[key] += val
                    pos[e] += 1
                    done += 1
                    progress = True
        if done != total:
            stuck = {e: (pos[e], len(self.trace[e]), self.trace[e][pos[e]]
                         if pos[e] < len(self.trace[e]) else None) for e in self.ENG}
            raise RuntimeError(f"DEADLOCK in schedule: sems={cnt} stuck={stuck}")

    def play(self):
        self.check()

        def runner(eng, fns):
            for i, f in enumerate(fns):
                try:
                    f(eng)
                except Exception as ex:
                    raise RuntimeError(
                        "emit failed at %s[%d]: %s" % (eng, i, getattr(f, "meta", "wait"))
                    ) from ex

        with self.nc.Block() as blk:
            for e in self.ENG:
                if self.q[e]:
                    getattr(blk, e)(lambda eng, fns=tuple(self.q[e]): runner(eng, fns))



def load_const_sb(C, ph, name, dram_key, shape, dt_, cast=False):
    t = C.sb(ph, "pc_" + name, shape, dt_)[:]
    C.do("gpsimd" if cast else "sync", "dma_start", inc=C.sc, n=16,
         out=t, in_=C.dram[dram_key])
    val = C.sc.n
    for e in ("tensor", "vector", "scalar"):
        C.w(e, C.sc, val)
    C.cons[name] = t
    return t


def build_bc_tile(C, ph, name, row_dram_key, width):
    nc = C.nc
    sm, sv = C.sm, C.sv
    row = C.sb(ph, "bcrow_" + name, [1, width], F32)[:]
    tile = C.sb(ph, "bctile_" + name, [128, width], F32)[:]
    C.do("sync", "dma_start", inc=C.sc, n=16, out=row, in_=C.dram[row_dram_key])
    rdy = C.sc.n
    with nc.psum_tensor("psbc_%d" % C._u, [128, 512], F32) as psI:
        C._u += 1
        C.w("tensor", C.sc, rdy)
        for j in range((width + 511) // 512):
            w_ = min(512, width - j * 512)
            C.w("tensor", sv, sv.n)
            C.do("tensor", "matmul", psI[:, 0:w_], C.cons["onesrf"],
                 row[:, j * 512:j * 512 + w_], start=True, stop=True, inc=sm)
            C.w("vector", sm, sm.n)
            C.do("vector", "tensor_copy", out=tile[:, j * 512:j * 512 + w_],
                 in_=psI[:, 0:w_], inc=sv)
    C.cons[name] = tile
    C.war[name + "_sv"] = sv.n
    return tile


def ln_cols(C, cols, base, ssq_done):
    sv, sa = C.sv, C.sa
    c = lambda j: cols[:, base + j:base + j + 1]
    C.w("vector", sa, ssq_done)
    C.do("vector", "tensor_scalar", c(2), c(0), 1.0 / D, None, ALU.mult, inc=sv)
    C.do("vector", "tensor_tensor", c(3), c(2), c(2), ALU.mult, inc=sv)
    C.do("vector", "tensor_scalar", c(4), c(1), 1.0 / D, c(3), ALU.mult, ALU.subtract, inc=sv)
    C.w("scalar", sv, sv.n)
    C.do("scalar", "activation", c(5), c(4), AF.Sqrt, bias=C.cons["fcol"][:, FCOL["eps"]:FCOL["eps"] + 1], inc=sa)
    C.w("vector", sa, sa.n)
    C.do("vector", "reciprocal", c(6), c(5), inc=sv)
    return c(2), c(6)


def emit_ln_to_xnT(C, x_src, xnT):
    sx, sm, sv, sa = C.sx, C.sm, C.sv, C.sa
    nc = C.nc
    ph = ExitStack()
    xt2 = C.sb(ph, "lnxt", [128, 2 * D], F32)
    xnb2 = C.sb(ph, "lnxnb", [128, 2 * D], BF16)
    sqs = C.sb(ph, "lnsqs", [128, D], BF16)
    ptA = C.ps(ph, "lnptA", [128, 1024], BF16)
    cols = C.bufs["cols"]
    cons_done = [0, 0]
    xnb_used = [0, 0]
    ptA_war = 0
    for i in range(NT):
        bi = i % 2
        xt = xt2[:, bi * D:(bi + 1) * D]
        xnb = xnb2[:, bi * D:(bi + 1) * D]
        C.w("sync", sv, cons_done[bi])
        C.do("sync", "dma_start", inc=C.sd[bi], n=16, out=xt,
             in_=x_src[i * 128:(i + 1) * 128, :])
        xload = C.sd[bi].n
        base = bi * 8
        c = lambda j: cols[:, base + j:base + j + 1]
        C.w("vector", C.sd[bi], xload)
        C.do("vector", "reduce_sum", c(0), xt, axis=AXX, inc=sv)
        C.w("scalar", C.sd[bi], xload)
        C.w("scalar", sv, cons_done[bi])
        C.do("scalar", "activation", sqs[:], xt, AF.Square, accum_out=c(1), inc=sa)
        mu, rstd = ln_cols(C, cols, base, sa.n)
        C.w("vector", sm, xnb_used[bi])
        C.do("vector", "tensor_scalar", xnb, xt, mu, rstd, ALU.subtract, ALU.mult, inc=sv)
        cons_done[bi] = sv.n
        xnb_done = sv.n
        for g in range(2):
            C.w("tensor", sv, xnb_done)
            C.w("tensor", sv, ptA_war)
            for j in range(8):
                dc = g * 8 + j
                C.do("tensor", "transpose", ptA[:, j * 128:(j + 1) * 128],
                     xnb[:, dc * 128:(dc + 1) * 128], C.cons["idn"],
                     inc=(sm if j == 7 else None))
            C.w("vector", sm, sm.n)
            dst = xnT.rearrange("p (dc t) -> p dc t", dc=16)[:, g * 8:(g + 1) * 8,
                                                            i * 128:(i + 1) * 128]
            C.do("vector", "tensor_copy", out=dst,
                 in_=ptA[:].rearrange("p (j t) -> p j t", j=8), inc=sv)
            ptA_war = sv.n
        xnb_used[bi] = sm.n
    ph.close()
    return sv.n


def emit_post_ln_residual(C, ybf_tile, x_src_tile, x_dst_tile, slot, y_ready_val):
    sx, sv, sa = C.sx, C.sv, C.sa
    b = C.bufs
    xt = b["xt3"][:, slot * D:(slot + 1) * D]
    cols, sqs = b["cols2"], b["sqs2"]
    base = slot * 8
    c = lambda j: cols[:, base + j:base + j + 1]
    sd = C.sd[2 + slot]
    C.w("sync", sd, C.war.get("res_store%d" % slot, 0))
    C.do("sync", "dma_start", inc=sd, n=16, out=xt, in_=x_src_tile)
    xload = sd.n
    C.w("vector", sv, y_ready_val)
    C.do("vector", "reduce_sum", c(0), ybf_tile, axis=AXX, inc=sv)
    C.w("scalar", sv, y_ready_val)
    C.w("scalar", sv, C.war.get("res_cons%d" % slot, 0))
    C.do("scalar", "activation", sqs, ybf_tile, AF.Square, accum_out=c(1), inc=sa)
    C.war["res_sq"] = sa.n
    mu, rstd = ln_cols(C, cols, base, sa.n)
    C.do("vector", "tensor_scalar", c(7), mu, rstd, -1.0, ALU.mult, ALU.mult, inc=sv)
    C.w("vector", sd, xload)
    C.do("vector", "tensor_scalar", xt, xt, c(7), None, ALU.add, inc=sv)
    C.do("vector", "scalar_tensor_tensor", xt, ybf_tile, rstd, xt, ALU.mult, ALU.add, inc=sv)
    C.war["res_cons%d" % slot] = sv.n
    C.w("sync", sv, sv.n)
    C.do("sync", "dma_start", inc=sd, n=16, out=x_dst_tile, in_=xt)
    C.war["res_store%d" % slot] = sd.n


def emit_block_a(C, b, x_in, x_out):
    nc = C.nc
    sx, sw, sm, sv, sa = C.sx, C.sw, C.sm, C.sv, C.sa
    cons = C.cons

    ph0 = ExitStack()
    xnT = C.sb(ph0, "xnT", [128, 16 * 2048], BF16)[:]
    load_const_sb(C, ph0, "ropeC", "c_ropeC", [64, 2048], BF16)
    load_const_sb(C, ph0, "ropeS", "c_ropeS", [64, 2048], BF16)
    C.barrier()
    xnt_ready = emit_ln_to_xnT(C, x_in[b], xnT)

    C.barrier()
    ph = ExitStack()
    wqkv_sb2 = C.sb(ph, "a2w", [128, 2 * 3 * 2048], BF16)
    qkv_sb = C.sb(ph, "a2qkv", [128, 3 * 2048], BF16)
    vtok = C.sb(ph, "a2vtok", [128, 2048], BF16)
    rrot = C.sb(ph, "a2rrot", [64, 2048], BF16)
    PT = C.sb(ph, "a2PT", [128, 16 * 512], BF16)
    drec = C.sb(ph, "a2drec", [1, 2048], F32)
    bcb = C.sb(ph, "a2bcb", [128, 512], F32)
    attnh2 = C.sb(ph, "a2attnh", [128, 2 * 2048], BF16)
    psA = C.ps(ph, "a2psA", [128, 2 * 512], F32)
    psS = C.ps(ph, "a2psS", [128, 2 * 1024], F32)
    psD = C.ps(ph, "a2psD", [1, 512], F32)
    psO = C.ps(ph, "a2psO", [128, 512], F32)
    ptA = psA[:].bitcast(BF16)

    qT = qkv_sb[:, 0:2048]
    kT = qkv_sb[:, 2048:4096]
    vT = qkv_sb[:, 4096:6144]
    w_used = [0, 0]
    attn_stored = [0, 0]
    psA_war = [0, 0]
    qkv_tensor_done = 0
    vtok_tensor_done = 0
    attn_evac = 0
    exp_done = [0, 0]
    for t3 in range(3):
        C.do("gpsimd", "dma_start", inc=C.sd[2], n=16,
             out=wqkv_sb2[:, t3 * 2048:(t3 + 1) * 2048],
             in_=C.dram["wqkv"][t3, 0].rearrange("p dc f -> p (dc f)"))
    w_ready = [C.sd[2].n, 0]
    for h in range(H):
        hb = h % 2
        wq_off = hb * 6144
        if h + 1 < H:
            nb = (h + 1) % 2
            C.w("gpsimd", sm, w_used[nb])
            for t3 in range(3):
                C.do("gpsimd", "dma_start", inc=C.sd[2 + nb], n=16,
                     out=wqkv_sb2[:, nb * 6144 + t3 * 2048:nb * 6144 + (t3 + 1) * 2048],
                     in_=C.dram["wqkv"][t3, h + 1].rearrange("p dc f -> p (dc f)"))
            w_ready[nb] = C.sd[2 + nb].n
        for t3 in range(3):
            for qc in range(4):
                pb = (t3 * 4 + qc) % 2
                if t3 == 0 and qc == 0:
                    C.w("tensor", C.sd[2 + hb], w_ready[hb])
                    C.w("tensor", sv, xnt_ready)
                C.w("tensor", sv, psA_war[pb])
                for dc in range(16):
                    C.do("tensor", "matmul", psA[:, pb * 512:(pb + 1) * 512],
                         wqkv_sb2[:, wq_off + t3 * 2048 + dc * 128:
                                  wq_off + t3 * 2048 + (dc + 1) * 128],
                         xnT[:, dc * 2048 + qc * 512:dc * 2048 + (qc + 1) * 512],
                         start=(dc == 0), stop=(dc == 15),
                         inc=(sm if dc == 15 else None))
                mm_done = sm.n
                C.w("vector", sm, mm_done)
                if t3 == 0 and qc == 0:
                    C.w("vector", sm, qkv_tensor_done)
                C.do("vector", "tensor_scalar",
                     qkv_sb[:, t3 * 2048 + qc * 512:t3 * 2048 + (qc + 1) * 512],
                     psA[:, pb * 512:(pb + 1) * 512],
                     C.cons["bqkv"][:, h * 3 + t3:h * 3 + t3 + 1], None,
                     ALU.add, inc=sv)
                psA_war[pb] = sv.n
        w_used[hb] = sm.n
        qkv_done = sv.n
        for tt in (qT, kT):
            C.do("vector", "tensor_copy", out=rrot[0:32, :], in_=tt[32:64, :], inc=sv)
            C.do("vector", "tensor_copy", out=rrot[32:64, :], in_=tt[0:32, :], inc=sv)
            C.do("vector", "tensor_tensor", rrot[0:64, :], rrot[0:64, :],
                 cons["ropeS"][0:64, :], ALU.mult, inc=sv)
            C.do("vector", "tensor_tensor", tt[0:64, :], tt[0:64, :],
                 cons["ropeC"][0:64, :], ALU.mult, inc=sv)
            C.do("vector", "tensor_tensor", tt[0:64, :], tt[0:64, :],
                 rrot[0:64, :], ALU.add, inc=sv)
        rope_done = sv.n
        for g in range(2):
            C.w("tensor", sv, qkv_done)
            C.w("tensor", sv, psA_war[g])
            for j in range(8):
                kt = g * 8 + j
                C.do("tensor", "transpose",
                     ptA[:, g * 1024 + j * 128:g * 1024 + (j + 1) * 128],
                     vT[:, kt * 128:(kt + 1) * 128], cons["idn"],
                     inc=(sm if j == 7 else None))
            C.w("vector", sm, sm.n)
            C.w("vector", sm, vtok_tensor_done)
            C.do("vector", "tensor_copy", out=vtok[:, g * 1024:(g + 1) * 1024],
                 in_=ptA[:, g * 1024:(g + 1) * 1024], inc=sv)
            psA_war[g] = sv.n
        vtok_done = sv.n
        for qc in range(4):
            for sg in range(8):
                sb_ = sg % 2
                if sg == 0:
                    C.w("tensor", sv, rope_done)
                    C.w("tensor", sv, attn_evac)
                C.w("tensor", sa, exp_done[sb_])
                for j in range(2):
                    kt = sg * 2 + j
                    C.do("tensor", "matmul",
                         psS[:, sb_ * 1024 + j * 512:sb_ * 1024 + (j + 1) * 512],
                         kT[:, kt * 128:(kt + 1) * 128],
                         qT[:, qc * 512:(qc + 1) * 512],
                         start=True, stop=True, inc=(sm if j == 1 else None))
                s_done = sm.n
                C.w("scalar", sm, s_done)
                C.do("scalar", "activation", PT[:, sg * 1024:(sg + 1) * 1024],
                     psS[:, sb_ * 1024:(sb_ + 1) * 1024], AF.Exp,
                     bias=C.cons["fcol"][:, FCOL["nshift"]:FCOL["nshift"] + 1],
                     scale=A0_SCALE, inc=sa)
                exp_done[sb_] = sa.n
            all_exp = sa.n
            C.w("tensor", sa, all_exp)
            C.w("tensor", sv, vtok_done)
            for kt in range(16):
                C.do("tensor", "matmul", psD[:], cons["onesc"],
                     PT[:, kt * 512:(kt + 1) * 512], start=(kt == 0), stop=(kt == 15),
                     inc=(sm if kt == 15 else None))
            den_done = sm.n
            for kt in range(16):
                C.do("tensor", "matmul", psO[:], vtok[:, kt * 128:(kt + 1) * 128],
                     PT[:, kt * 512:(kt + 1) * 512], start=(kt == 0), stop=(kt == 15),
                     inc=(sm if kt == 15 else None))
            pv_done = sm.n
            vtok_tensor_done = pv_done
            C.w("vector", sm, den_done)
            C.do("vector", "reciprocal", drec[:, qc * 512:(qc + 1) * 512], psD[:], inc=sv)
            rec_done = sv.n
            C.w("tensor", sv, rec_done)
            C.w("tensor", sv, psA_war[0])
            C.do("tensor", "matmul", psA[:, 0:512], cons["onesrf"],
                 drec[:, qc * 512:(qc + 1) * 512], start=True, stop=True, inc=sm)
            bc_done = sm.n
            C.w("vector", sm, bc_done)
            C.do("vector", "tensor_copy", out=bcb[:], in_=psA[:, 0:512], inc=sv)
            psA_war[0] = sv.n
            C.w("vector", sm, pv_done)
            C.w("vector", C.sd[hb], attn_stored[hb])
            C.do("vector", "tensor_tensor",
                 attnh2[:, hb * 2048 + qc * 512:hb * 2048 + (qc + 1) * 512],
                 psO[:], bcb[:], ALU.mult, inc=sv)
            attn_evac = sv.n
        qkv_tensor_done = sm.n
        C.w("sync", sv, attn_evac)
        C.do("sync", "dma_start", inc=C.sd[hb], n=16, out=C.dram["attnT"][b, h],
             in_=attnh2[:, hb * 2048:(hb + 1) * 2048])
        attn_stored[hb] = C.sd[hb].n
    a2_end_sx = sx.n
    a2_end_d = [C.sd[0].n, C.sd[1].n]
    ph.close()
    ph0.close()

    emit_out_proj(C, b, C.dram["attnT"], C.dram["wout"], "bout", x_in, x_out)


def emit_out_proj(C, b, src_scr, wout_d, bias_key, x_in, x_out):
    nc = C.nc
    sx, sw, sm, sv, sa = C.sx, C.sw, C.sm, C.sv, C.sa
    cons = C.cons
    use_bias = bias_key is not None
    a2_end_sx = sx.n
    C.barrier()
    ph = ExitStack()
    if use_bias:
        build_bc_tile(C, ph, "boutbc", bias_key, 2048)
    C.bufs["xt3"] = C.sb(ph, "a3xt3", [128, 2 * D], F32)[:]
    C.bufs["sqs2"] = C.sb(ph, "a3sqs2", [128, D], BF16)[:]
    atsl = C.sb(ph, "a3atsl", [128, 16 * 512], BF16)
    wo2 = C.sb(ph, "a3wo", [128, 2 * 2048], BF16)
    ptmp = C.sb(ph, "a3ptmp", [128, 512], BF16)
    projtm = C.sb(ph, "a3projtm", [128, 4 * 2048], BF16)
    psP = C.ps(ph, "a3psP", [128, 2 * 512], F32)
    psT = C.ps(ph, "a3psT", [128, 512], BF16)
    atsl_war = 0
    wo_used = [0, 0]
    evac_sv = [0, 0]
    psT_war = 0
    projtm_used = 0
    for tc in range(4):
        C.w("scalar", sm, atsl_war)
        if tc == 0:
            C.w("scalar", sx, a2_end_sx)
            for _i in range(4):
                C.w("scalar", C.sd[_i], C.sd[_i].n)
        src = src_scr[b][:, :, tc * 512:(tc + 1) * 512].rearrange("h p q -> p h q")
        C.do("scalar", "dma_start", inc=C.sy, n=16,
             out=atsl.rearrange("p (h q) -> p h q", h=16), in_=src)
        atsl_ready = C.sy.n
        for dt in range(16):
            wb = dt % 2
            C.w("gpsimd", sm, wo_used[wb])
            C.do("gpsimd", "dma_start", inc=C.sd[wb], n=16,
                 out=wo2[:, wb * 2048:(wb + 1) * 2048], in_=wout_d[dt])
            wo_ready = C.sd[wb].n
            pb = dt % 2
            C.w("tensor", C.sd[wb], wo_ready)
            if dt == 0:
                C.w("tensor", C.sy, atsl_ready)
            C.w("tensor", sv, evac_sv[pb])
            for dc in range(16):
                C.do("tensor", "matmul", psP[:, pb * 512:(pb + 1) * 512],
                     wo2[:, wb * 2048 + dc * 128:wb * 2048 + (dc + 1) * 128],
                     atsl[:, dc * 512:(dc + 1) * 512],
                     start=(dc == 0), stop=(dc == 15), inc=(sm if dc == 15 else None))
            wo_used[wb] = sm.n
            proj_done = sm.n
            C.w("vector", sm, proj_done)
            C.do("vector", "tensor_copy", out=ptmp[:], in_=psP[:, pb * 512:(pb + 1) * 512],
                 inc=sv)
            evac_sv[pb] = sv.n
            ptmp_done = sv.n
            C.w("tensor", sv, ptmp_done)
            C.w("tensor", sv, psT_war)
            for j in range(4):
                C.do("tensor", "transpose", psT[:, j * 128:(j + 1) * 128],
                     ptmp[:, j * 128:(j + 1) * 128], cons["idn"],
                     inc=(sm if j == 3 else None))
            tt_done = sm.n
            C.w("vector", sm, tt_done)
            if dt == 0:
                C.w("vector", sv, projtm_used)
                C.w("vector", sa, C.war.get("res_sq", 0))
                C.w("vector", sv, C.war.get("boutbc_sv", 0))
            dst = projtm.rearrange("p (j n) -> p j n", j=4)[:, :, dt * 128:(dt + 1) * 128]
            if use_bias:
                C.do("vector", "scalar_tensor_tensor", dst,
                     psT.rearrange("p (j n) -> p j n", j=4), 1.0,
                     cons["boutbc"][:, dt * 128:(dt + 1) * 128].unsqueeze(1).broadcast_to(
                         [128, 4, 128]),
                     ALU.mult, ALU.add, inc=sv)
            else:
                C.do("vector", "tensor_copy", out=dst,
                     in_=psT.rearrange("p (j n) -> p j n", j=4), inc=sv)
            psT_war = sv.n
        proj_ready = sv.n
        for j in range(4):
            i = tc * 4 + j
            emit_post_ln_residual(C, projtm[:, j * 2048:(j + 1) * 2048],
                                  x_in[b][i * 128:(i + 1) * 128, :],
                                  x_out[b][i * 128:(i + 1) * 128, :],
                                  j % 2, proj_ready)
        projtm_used = sv.n
        atsl_war = sm.n
    ph.close()


L1S = 128 ** -0.5
CPAD = 16
KLAM, KLAMP, KBS2N, KBETA, KNLAM, KNGPS, KGT, KGC = range(8)


def emit_block_c(C, b, x_in, x_out):
    nc = C.nc
    sx, sw, sm, sv, sa, sy = C.sx, C.sw, C.sm, C.sv, C.sa, C.sy
    cons = C.cons

    ph0 = ExitStack()
    lamT = C.sb(ph0, "c_lamT", [16, 2048], F32)[:]
    lampT = C.sb(ph0, "c_lampT", [16, 2048], F32)[:]
    bs2nT = C.sb(ph0, "c_bs2nT", [16, 2048], F32)[:]
    betaT = C.sb(ph0, "c_betaT", [16, 2048], F32)[:]
    lnaT = C.sb(ph0, "c_lnaT", [16, 2048], F32)[:]
    alphaT = C.sb(ph0, "c_alphaT", [16, 2048], F32)[:]
    colstore = C.sb(ph0, "c_cols", [128, 16 * 128], F32)[:]
    col = lambda cc, k: colstore[:, cc * 128 + k * 16:cc * 128 + (k + 1) * 16]
    load_const_sb(C, ph0, "selF", "c_selF", [16, 2048], F32)
    build_bc_tile(C, ph0, "lnwbc", "c_lnw", 128)
    build_bc_tile(C, ph0, "lnbbc", "c_lnb", 128)
    phX = ExitStack()
    xnT = C.sb(phX, "xnTc", [128, 16 * 2048], BF16)[:]

    C.barrier()
    xnt_ready = emit_ln_to_xnT(C, x_in[b], xnT)

    C.barrier()
    ph = ExitStack()
    psZ = C.ps(ph, "c1psZ", [16, 2 * 512], F32)
    psC = C.ps(ph, "c1psC", [128, 128], F32)
    psG = C.ps(ph, "c1psG", [128, 16], F32)
    psZ_war = 0
    for qc in range(4):
        sl = slice(qc * 512, (qc + 1) * 512)
        if qc == 0:
            C.w("tensor", sv, xnt_ready)
        C.w("tensor", sa, psZ_war)
        for z, woff in ((0, 0), (1, 256)):
            for dc in range(16):
                C.do("tensor", "matmul", psZ[:, z * 512:(z + 1) * 512],
                     cons["wab"][:, woff + dc * 16:woff + (dc + 1) * 16],
                     xnT[:, dc * 2048 + qc * 512:dc * 2048 + (qc + 1) * 512],
                     start=(dc == 0), stop=(dc == 15), inc=(sm if dc == 15 else None))
        zdone = sm.n
        C.w("scalar", sm, zdone)
        C.do("scalar", "activation", alphaT[:, sl], psZ[:, 0:512], AF.Sigmoid,
             bias=cons["bab"][:, 0:1], inc=sa)
        C.do("scalar", "activation", betaT[:, sl], psZ[:, 512:1024], AF.Sigmoid,
             bias=cons["bab"][:, 1:2], inc=sa)
        psZ_war = sa.n
        C.do("scalar", "activation", lnaT[:, sl], alphaT[:, sl], AF.Ln, inc=sa)
        ln_done = sa.n
        C.w("vector", sa, ln_done)
        C.do("vector", "tensor_scalar", bs2nT[:, sl], betaT[:, sl],
             -(L1S * L1S), None, ALU.mult, inc=sv)
        for c in range(qc * 4, (qc + 1) * 4):
            csl = slice(c * 128, (c + 1) * 128)
            C.do("vector", "tensor_tensor_scan", lamT[:, csl], lnaT[:, csl],
                 lnaT[:, csl], 0.0, ALU.add, ALU.bypass, inc=sv)
            C.do("vector", "tensor_sub", lampT[:, csl], lamT[:, csl], lnaT[:, csl],
                 inc=sv)
        lam_done = sv.n
        for c in range(qc * 4, (qc + 1) * 4):
            csl = slice(c * 128, (c + 1) * 128)
            C.w("tensor", sv, lam_done)
            C.w("tensor", sv, C.war.get("c1_psC", 0))
            for k, tsrc in ((KLAM, lamT), (KLAMP, lampT), (KBS2N, bs2nT), (KBETA, betaT)):
                C.do("tensor", "transpose", psC[:, k * 16:(k + 1) * 16],
                     tsrc[:, csl], cons["idnf"][0:16, 0:16],
                     inc=(sm if k == KBETA else None))
            tdone = sm.n
            C.w("vector", sm, tdone)
            C.do("vector", "tensor_copy", out=colstore[:, c * 128:c * 128 + 64],
                 in_=psC[:, 0:64], inc=sv)
            C.war["c1_psC"] = sv.n
            C.do("vector", "tensor_scalar", col(c, KNLAM), col(c, KLAM), -1.0, None,
                 ALU.mult, inc=sv)
            C.w("scalar", sv, sv.n)
            C.do("scalar", "activation", col(c, KNGPS), col(c, KLAMP), AF.Exp,
                 bias=cons["fcol"][:, FCOL["lns1"]:FCOL["lns1"] + 1], inc=sa)
            C.do("scalar", "activation", col(c, KGT), col(c, KLAM), AF.Exp, inc=sa)
            C.w("vector", sa, sa.n)
            C.do("vector", "tensor_scalar", col(c, KNGPS), col(c, KNGPS), -1.0, None,
                 ALU.mult, inc=sv)
            C.w("tensor", sv, sv.n)
            C.w("tensor", sa, C.war.get("c1_psG", 0))
            for hh in range(16):
                C.do("tensor", "matmul", psG[:, hh:hh + 1],
                     cons["selF"][:, hh * 128:(hh + 1) * 128],
                     lamT[:, c * 128 + 127:c * 128 + 128],
                     start=True, stop=True, inc=(sm if hh == 15 else None))
            C.w("scalar", sm, sm.n)
            C.do("scalar", "activation", col(c, KGC), psG[:], AF.Exp, inc=sa)
            C.war["c1_psG"] = sa.n
    ph.close()

    C.barrier()
    ph = ExitStack()
    w4 = C.sb(ph, "c2w", [128, 4 * 2048], BF16)[:]
    CW = CPAD + 2048
    qraw = C.sb(ph, "c2qraw", [128, 3 * CW], BF16)[:]
    ctmp = C.sb(ph, "c2ctmp", [128, 2048], F32)[:]
    qkT = C.sb(ph, "c2qk", [128, 2 * 2048], BF16)[:]
    vsT = C.sb(ph, "c2vs", [128, 2048], BF16)[:]
    gT = C.sb(ph, "c2gT", [128, 2048], BF16)[:]
    vtm = C.sb(ph, "c2vtm", [128, 2048], BF16)[:]
    gat = C.sb(ph, "c2gat", [128, 2048], BF16)[:]
    psA = C.ps(ph, "c2psA", [128, 2 * 512], F32)
    ptC = C.ps(ph, "c2ptC", [128, 1024], BF16)
    qkvg = C.dram["qkvg_scr"]
    C.do("vector", "memset", qraw[:, 0:CPAD], 0.0)
    C.do("vector", "memset", qraw[:, CW:CW + CPAD], 0.0)
    C.do("vector", "memset", qraw[:, 2 * CW:2 * CW + CPAD], 0.0)
    w_used = 0
    psA_war_v = [0, 0]
    psA_war_a = [0, 0]
    store_done = {}
    gT_used = 0
    ptC_war = 0
    for h in range(H):
        C.w("gpsimd", sm, w_used)
        for t4 in range(4):
            C.do("gpsimd", "dma_start", inc=sw, n=16,
                 out=w4[:, t4 * 2048:(t4 + 1) * 2048],
                 in_=C.dram["l1w"][t4, h].rearrange("p dc f -> p (dc f)"))
        w_ready = sw.n
        raw_done = [0] * 4
        for t4 in range(4):
            for qc in range(4):
                pb = (t4 * 4 + qc) % 2
                if t4 == 0 and qc == 0:
                    C.w("tensor", sw, w_ready)
                    if h == 0:
                        C.w("tensor", sv, xnt_ready)
                C.w("tensor", sa, psA_war_a[pb])
                C.w("tensor", sv, psA_war_v[pb])
                for dc in range(16):
                    C.do("tensor", "matmul", psA[:, pb * 512:(pb + 1) * 512],
                         w4[:, t4 * 2048 + dc * 128:t4 * 2048 + (dc + 1) * 128],
                         xnT[:, dc * 2048 + qc * 512:dc * 2048 + (qc + 1) * 512],
                         start=(dc == 0), stop=(dc == 15), inc=(sm if dc == 15 else None))
                mmd = sm.n
                if t4 < 3:
                    C.w("vector", sm, mmd)
                    C.do("vector", "tensor_copy",
                         out=qraw[:, t4 * CW + CPAD + qc * 512:
                                  t4 * CW + CPAD + (qc + 1) * 512],
                         in_=psA[:, pb * 512:(pb + 1) * 512], inc=sv)
                    psA_war_v[pb] = sv.n
                    raw_done[t4] = sv.n
                else:
                    C.w("scalar", sm, mmd)
                    C.w("scalar", sm, gT_used)
                    C.do("scalar", "activation", gT[:, qc * 512:(qc + 1) * 512],
                         psA[:, pb * 512:(pb + 1) * 512], AF.Sigmoid, inc=sa)
                    psA_war_a[pb] = sa.n
                    raw_done[t4] = sa.n
        w_used = sm.n
        for t4, dstap in ((0, qkT[:, 0:2048]), (1, qkT[:, 2048:4096]), (2, vsT[:])):
            base = t4 * CW
            C.w("vector", sv, raw_done[t4])
            C.w("vector", sa, sa.n)
            C.do("vector", "tensor_scalar", ctmp[:], qraw[:, base + 13:base + 13 + 2048],
                 cons["cw"][:, t4 * 64 + h * 4:t4 * 64 + h * 4 + 1], None, ALU.mult, inc=sv)
            for s in range(1, 4):
                C.do("vector", "scalar_tensor_tensor", ctmp[:],
                     qraw[:, base + 13 + s:base + 13 + s + 2048],
                     cons["cw"][:, t4 * 64 + h * 4 + s:t4 * 64 + h * 4 + s + 1],
                     ctmp[:], ALU.mult, ALU.add, inc=sv)
            C.w("scalar", sv, sv.n)
            C.w("scalar", C.sd[t4], store_done.get((h - 1, t4), 0))
            C.do("scalar", "activation", dstap, ctmp[:], AF.Silu,
                 bias=cons["cb"][:, t4 * 16 + h:t4 * 16 + h + 1], inc=sa)
        conv_done = sa.n
        for which, (src_sb, dst_sb) in enumerate(((vsT, vtm), (gT, gat))):
            for g in range(2):
                C.w("tensor", sa, conv_done if which == 0 else raw_done[3])
                C.w("tensor", sv, ptC_war)
                for j in range(8):
                    kt = g * 8 + j
                    C.do("tensor", "transpose", ptC[:, j * 128:(j + 1) * 128],
                         src_sb[:, kt * 128:(kt + 1) * 128], cons["idn"],
                         inc=(sm if j == 7 else None))
                tdone = sm.n
                C.w("vector", sm, tdone)
                C.w("vector", C.sd[2 + which], store_done.get((h - 1, 2 + which), 0))
                C.do("vector", "tensor_copy", out=dst_sb[:, g * 1024:(g + 1) * 1024],
                     in_=ptC[:], inc=sv)
                ptC_war = sv.n
            if which == 1:
                gT_used = sm.n
        tm_done = sv.n
        C.w("sync", sa, conv_done)
        C.w("sync", sv, tm_done)
        for idx, src_sb in ((0, qkT[:, 0:2048]), (1, qkT[:, 2048:4096]),
                            (2, vtm[:]), (3, gat[:])):
            C.do("sync", "dma_start", inc=C.sd[idx], n=16, out=qkvg[idx, h], in_=src_sb)
            store_done[(h, idx)] = C.sd[idx].n
    ph.close()
    phX.close()

    C.barrier()
    emit_c_scan(C, b, colstore, lamT, lampT, bs2nT, col)
    ph0.close()

    C.barrier()
    emit_out_proj(C, b, C.dram["ovT_scr"], C.dram["l1wo"], None, x_in, x_out)


def emit_c_scan(C, b, colstore, lamT, lampT, bs2nT, col):
    nc = C.nc
    sx, sw, sm, sv, sa, sy = C.sx, C.sw, C.sm, C.sv, C.sa, C.sy
    cons = C.cons
    qkvg = C.dram["qkvg_scr"]
    ovscr = C.dram["ovT_scr"]

    ph = ExitStack()
    qT4 = C.sb(ph, "sc_q", [128, 4 * 2048], BF16)[:]
    kT4 = C.sb(ph, "sc_k", [128, 4 * 2048], BF16)[:]
    vtm4 = C.sb(ph, "sc_v", [128, 4 * 2048], BF16)[:]
    gat4 = C.sb(ph, "sc_g", [128, 4 * 2048], BF16)[:]
    ovT4 = C.sb(ph, "sc_ov", [128, 4 * 2048], BF16)[:]
    S4f = C.sb(ph, "sc_Sf", [128, 512], F32)[:]
    S4b = C.sb(ph, "sc_Sb", [128, 512], BF16)[:]
    e1m = C.sb(ph, "sc_e1", [128, 512], BF16)[:]
    e1mT = C.sb(ph, "sc_e1T", [128, 512], BF16)[:]
    e2mT = C.sb(ph, "sc_e2T", [128, 512], BF16)[:]
    ecl = C.sb(ph, "sc_ecl", [128, 512], F32)[:]
    bb4 = C.sb(ph, "sc_bb4", [128, 512], F32)[:]
    bcf = C.sb(ph, "sc_bcf", [128, 512], F32)[:]
    Xp = C.sb(ph, "sc_X", [128, 2 * 512], BF16)[:]
    Xn = C.sb(ph, "sc_Xn", [128, 2 * 512], BF16)[:]
    U4 = C.sb(ph, "sc_U", [128, 512], BF16)[:]
    RHS4 = C.sb(ph, "sc_RHS", [128, 512], F32)[:]
    MT4 = C.sb(ph, "sc_MT", [128, 512], BF16)[:]
    O4 = C.sb(ph, "sc_O", [128, 512], F32)[:]
    t4b = C.sb(ph, "sc_t4b", [128, 512], F32)[:]
    ovc = C.sb(ph, "sc_ovc", [128, 512], BF16)[:]
    Ktm4 = C.sb(ph, "sc_Ktm", [128, 512], BF16)[:]
    cst = C.sb(ph, "sc_cst", [128, 48], F32)[:]
    PA = C.ps(ph, "sc_PA", [128, 512], F32)
    PB = C.ps(ph, "sc_PB", [128, 512], F32)
    PC = C.ps(ph, "sc_PC", [128, 512], F32)
    PD = C.ps(ph, "sc_PD", [128, 512], F32)
    PE = C.ps(ph, "sc_PE", [128, 512], F32)
    PF = C.ps(ph, "sc_PF", [128, 512], F32)
    PG = C.ps(ph, "sc_PG", [128, 512], F32)
    PH = C.ps(ph, "sc_PH", [128, 1024], BF16)

    def bc4(dst, srccols):
        C.do("vector", "tensor_copy",
             out=dst.rearrange("p (a z) -> p a z", a=4),
             in_=srccols.unsqueeze(2).broadcast_to([128, 4, 128]), inc=sv)

    m3 = lambda ap: ap.rearrange("p (a z) -> p a z", a=4)
    mask3 = lambda m: m.unsqueeze(1).broadcast_to([128, 4, 128])

    ld_used = 0
    S_used = 0
    ov_stored = {}
    for hg in range(4):
        hs = [hg * 4 + i for i in range(4)]
        C.w("sync", sm, ld_used)
        for i, h in enumerate(hs):
            for t4 in range(4):
                dst = (qT4, kT4, vtm4, gat4)[t4]
                C.do("sync", "dma_start", inc=sy, n=16,
                     out=dst[:, i * 2048:(i + 1) * 2048], in_=qkvg[t4, h])
        ld_ready = sy.n
        C.w("vector", sm, S_used)
        C.do("vector", "memset", S4f[:], 0.0, inc=sv)
        C.do("vector", "memset", S4b[:], 0.0, inc=sv)
        zdone = sv.n
        war = {}
        kch = lambda i, c: kT4[:, i * 2048 + c * 128:i * 2048 + (c + 1) * 128]
        qch = lambda i, c: qT4[:, i * 2048 + c * 128:i * 2048 + (c + 1) * 128]
        for c in range(16):
            if c == 0:
                C.w("tensor", sy, ld_ready)
                C.w("tensor", sv, zdone)
            C.w("tensor", sv, war.get("PA_v", 0))
            C.w("tensor", sa, war.get("PA_a", 0))
            for i, h in enumerate(hs):
                C.do("tensor", "matmul", PA[:, i * 128:(i + 1) * 128],
                     cons["selF"][:, h * 128:(h + 1) * 128],
                     lamT[:, c * 128:(c + 1) * 128],
                     start=True, stop=True, inc=(sm if i == 3 else None))
            C.w("tensor", sv, war.get("PB_v", 0))
            for i, h in enumerate(hs):
                C.do("tensor", "matmul", PB[:, i * 128:(i + 1) * 128],
                     cons["selF"][:, h * 128:(h + 1) * 128],
                     lampT[:, c * 128:(c + 1) * 128],
                     start=True, stop=True, inc=(sm if i == 3 else None))
            C.w("tensor", sv, war.get("PC_v", 0))
            for i, h in enumerate(hs):
                C.do("tensor", "matmul", PC[:, i * 128:(i + 1) * 128],
                     cons["selF"][:, h * 128:(h + 1) * 128],
                     bs2nT[:, c * 128:(c + 1) * 128],
                     start=True, stop=True, inc=(sm if i == 3 else None))
            bc_done = sm.n
            C.w("vector", sm, bc_done)
            for i, h in enumerate(hs):
                C.do("vector", "tensor_scalar", ecl[:, i * 128:(i + 1) * 128],
                     PA[:, i * 128:(i + 1) * 128], col(c, KLAMP)[:, h:h + 1],
                     0.0, ALU.subtract, ALU.max, inc=sv)
            C.w("scalar", sv, sv.n)
            C.do("scalar", "activation", e1m[:], ecl[:], AF.Exp, scale=-1.0, inc=sa)
            C.w("vector", sa, sa.n)
            C.do("vector", "tensor_tensor", m3(e1m), m3(e1m), mask3(cons["mLTs"]),
                 ALU.mult, inc=sv, lp=True)
            for i, h in enumerate(hs):
                C.do("vector", "tensor_scalar", ecl[:, i * 128:(i + 1) * 128],
                     PB[:, i * 128:(i + 1) * 128], col(c, KLAM)[:, h:h + 1],
                     0.0, ALU.subtract, ALU.min, inc=sv)
            war["PB_v"] = sv.n
            C.w("scalar", sv, sv.n)
            C.do("scalar", "activation", e1mT[:], ecl[:], AF.Exp, inc=sa)
            C.w("vector", sa, sa.n)
            C.do("vector", "tensor_tensor", m3(e1mT), m3(e1mT), mask3(cons["mUTs"]),
                 ALU.mult, inc=sv, lp=True)
            for i, h in enumerate(hs):
                C.do("vector", "tensor_scalar", ecl[:, i * 128:(i + 1) * 128],
                     PA[:, i * 128:(i + 1) * 128], col(c, KLAM)[:, h:h + 1],
                     0.0, ALU.subtract, ALU.min, inc=sv)
            war["PA_v"] = sv.n
            C.w("scalar", sv, sv.n)
            C.do("scalar", "activation", e2mT[:], ecl[:], AF.Exp, inc=sa)
            war["PA_a"] = sa.n
            C.w("vector", sa, sa.n)
            C.do("vector", "tensor_tensor", m3(e2mT), m3(e2mT), mask3(cons["mUTis"]),
                 ALU.mult, inc=sv, lp=True)
            emat_done = sv.n
            C.w("vector", sm, bc_done)
            C.do("vector", "tensor_copy", out=bb4[:], in_=PC[:], inc=sv)
            war["PC_v"] = sv.n
            C.w("tensor", sv, war.get("PD_v", 0))
            for i in range(4):
                C.do("tensor", "matmul", PD[:, i * 128:(i + 1) * 128],
                     kch(i, c), kch(i, c), start=True, stop=True,
                     inc=(sm if i == 3 else None))
            kk_done = sm.n
            C.w("vector", sm, kk_done)
            C.w("vector", sv, emat_done)
            bc4(bcf, colstore[:, c * 128 + KBS2N * 16 + hg * 4:
                              c * 128 + KBS2N * 16 + hg * 4 + 4])
            C.do("vector", "tensor_tensor", ecl[:], PD[:], e1m[:], ALU.mult, inc=sv)
            C.do("vector", "tensor_tensor", Xp[:, 0:512], ecl[:], bcf[:], ALU.mult,
                 inc=sv, lp=True)
            C.do("vector", "tensor_tensor", ecl[:], PD[:], e1mT[:], ALU.mult, inc=sv)
            war["PD_v"] = sv.n
            C.do("vector", "tensor_tensor", Xp[:, 512:1024], ecl[:], bb4[:], ALU.mult,
                 inc=sv, lp=True)
            x_done = sv.n
            C.w("tensor", sv, war.get("PE_v", 0))
            for i in range(4):
                C.do("tensor", "matmul", PE[:, i * 128:(i + 1) * 128],
                     kch(i, c), S4b[:, i * 128:(i + 1) * 128], start=True, stop=True,
                     inc=(sm if i == 3 else None))
            ks_done = sm.n
            C.w("vector", sm, ks_done)
            for i, h in enumerate(hs):
                C.do("vector", "scalar_tensor_tensor", RHS4[:, i * 128:(i + 1) * 128],
                     PE[:, i * 128:(i + 1) * 128], col(c, KNGPS)[:, h:h + 1],
                     vtm4[:, i * 2048 + c * 128:i * 2048 + (c + 1) * 128],
                     ALU.mult, ALU.add, inc=sv)
            for i, h in enumerate(hs):
                C.do("vector", "tensor_scalar", U4[:, i * 128:(i + 1) * 128],
                     RHS4[:, i * 128:(i + 1) * 128], col(c, KBETA)[:, h:h + 1], None,
                     ALU.mult, inc=sv, lp=True)
            war["PE_v"] = sv.n
            u_init = sv.n
            C.w("tensor", sv, x_done)
            C.w("tensor", sv, u_init)
            xcur, xnxt = Xp, Xn
            pf_v = war.get("PF_v", 0)
            pg_v = war.get("PG_v", 0)
            for stage in range(7):
                C.w("tensor", sv, pf_v)
                for i in range(4):
                    C.do("tensor", "matmul", PF[:, i * 128:(i + 1) * 128],
                         xcur[:, 512 + i * 128:512 + (i + 1) * 128],
                         U4[:, i * 128:(i + 1) * 128], start=True, stop=True,
                         inc=(sm if i == 3 else None))
                C.w("vector", sm, sm.n)
                C.do("vector", "tensor_tensor", U4[:], U4[:], PF[:], ALU.add,
                     inc=sv, lp=True)
                pf_v = sv.n
                if stage < 6:
                    C.w("tensor", sv, pg_v)
                    for i in range(4):
                        C.do("tensor", "matmul", PG[:, i * 128:(i + 1) * 128],
                             xcur[:, 512 + i * 128:512 + (i + 1) * 128],
                             xcur[:, i * 128:(i + 1) * 128], start=True, stop=True,
                             inc=(sm if i == 3 else None))
                    C.w("vector", sm, sm.n)
                    C.do("vector", "tensor_copy", out=xnxt[:, 0:512], in_=PG[:], inc=sv)
                    x2_evac = sv.n
                    C.w("tensor", sv, x2_evac)
                    for i in range(4):
                        C.do("tensor", "matmul", PG[:, i * 128:(i + 1) * 128],
                             xcur[:, i * 128:(i + 1) * 128],
                             xcur[:, 512 + i * 128:512 + (i + 1) * 128], start=True,
                             stop=True, inc=(sm if i == 3 else None))
                    C.w("vector", sm, sm.n)
                    C.do("vector", "tensor_copy", out=xnxt[:, 512:1024], in_=PG[:],
                         inc=sv)
                    pg_v = sv.n
                    C.w("tensor", sv, pg_v)
                    xcur, xnxt = xnxt, xcur
            war["PF_v"] = pf_v
            war["PG_v"] = pg_v
            u_done = sv.n
            C.w("tensor", sv, u_done)
            C.w("tensor", sv, war.get("PG2_v", 0))
            for i in range(4):
                C.do("tensor", "matmul", PG[:, i * 128:(i + 1) * 128],
                     kch(i, c), qch(i, c), start=True, stop=True,
                     inc=(sm if i == 3 else None))
            kq_done = sm.n
            C.w("vector", sm, kq_done)
            C.do("vector", "tensor_tensor", MT4[:], PG[:], e2mT[:], ALU.mult,
                 inc=sv, lp=True)
            mt_done = sv.n
            C.w("tensor", sv, war.get("PE_v", 0))
            for i in range(4):
                C.do("tensor", "matmul", PE[:, i * 128:(i + 1) * 128],
                     qch(i, c), S4b[:, i * 128:(i + 1) * 128], start=True, stop=True,
                     inc=(sm if i == 3 else None))
            qs_done = sm.n
            C.w("tensor", sv, mt_done)
            for i in range(4):
                C.do("tensor", "matmul", PG[:, i * 128:(i + 1) * 128],
                     MT4[:, i * 128:(i + 1) * 128], U4[:, i * 128:(i + 1) * 128],
                     start=True, stop=True, inc=(sm if i == 3 else None))
            mu_done = sm.n
            C.w("vector", sm, mu_done)
            C.do("vector", "tensor_copy", out=t4b[:], in_=PG[:], inc=sv)
            war["PG2_v"] = sv.n
            C.w("vector", sm, qs_done)
            for i, h in enumerate(hs):
                C.do("vector", "scalar_tensor_tensor", O4[:, i * 128:(i + 1) * 128],
                     PE[:, i * 128:(i + 1) * 128], col(c, KGT)[:, h:h + 1],
                     t4b[:, i * 128:(i + 1) * 128], ALU.mult, ALU.add, inc=sv)
            war["PE_v"] = sv.n
            for i in range(4):
                C.do("vector", "reduce_sum", cst[:, i:i + 1],
                     O4[:, i * 128:(i + 1) * 128], axis=AXX, inc=sv)
            C.w("scalar", sv, sv.n)
            C.do("scalar", "activation", ecl[:], O4[:], AF.Square, inc=sa)
            C.w("vector", sa, sa.n)
            for i in range(4):
                C.do("vector", "reduce_sum", cst[:, 4 + i:5 + i],
                     ecl[:, i * 128:(i + 1) * 128], axis=AXX, inc=sv)
            C.do("vector", "tensor_scalar", cst[:, 8:12], cst[:, 0:4], 1.0 / 128, None,
                 ALU.mult, inc=sv)
            C.do("vector", "tensor_tensor", cst[:, 12:16], cst[:, 8:12], cst[:, 8:12],
                 ALU.mult, inc=sv)
            C.do("vector", "tensor_scalar", cst[:, 28:32], cst[:, 4:8], 1.0 / 128,
                 None, ALU.mult, inc=sv)
            C.do("vector", "tensor_sub", cst[:, 16:20], cst[:, 28:32], cst[:, 12:16],
                 inc=sv)
            C.w("scalar", sv, sv.n)
            C.do("scalar", "activation", cst[:, 20:24], cst[:, 16:20], AF.Sqrt,
                 bias=cons["fcol"][:, FCOL["eps"]:FCOL["eps"] + 1], inc=sa)
            C.w("vector", sa, sa.n)
            C.do("vector", "reciprocal", cst[:, 24:28], cst[:, 20:24], inc=sv)
            mu_bc = cst[:, 8:12].unsqueeze(2).broadcast_to([128, 4, 128])
            rstd_bc = cst[:, 24:28].unsqueeze(2).broadcast_to([128, 4, 128])
            gview = gat4[:].rearrange("p (a t) -> p a t", a=4)[:, :, c * 128:(c + 1) * 128]
            C.do("vector", "tensor_tensor", m3(O4), m3(O4), mu_bc, ALU.subtract, inc=sv)
            C.do("vector", "tensor_tensor", m3(O4), m3(O4), rstd_bc, ALU.mult, inc=sv)
            C.do("vector", "tensor_tensor", m3(O4), m3(O4), mask3(cons["lnwbc"]),
                 ALU.mult, inc=sv)
            C.do("vector", "tensor_tensor", m3(O4), m3(O4), mask3(cons["lnbbc"]),
                 ALU.add, inc=sv)
            C.w("vector", sm, war.get("ovc_m", 0))
            C.do("vector", "tensor_tensor", m3(ovc), m3(O4), gview, ALU.mult,
                 inc=sv, lp=True)
            ov_ready = sv.n
            C.w("tensor", sv, ov_ready)
            C.w("tensor", sv, war.get("PH_v", 0))
            for i in range(4):
                C.do("tensor", "transpose", PH[:, i * 128:(i + 1) * 128],
                     ovc[:, i * 128:(i + 1) * 128], cons["idn"],
                     inc=(sm if i == 3 else None))
            war["ovc_m"] = sm.n
            for i in range(4):
                C.do("tensor", "transpose", PH[:, 512 + i * 128:512 + (i + 1) * 128],
                     kch(i, c), cons["idn"], inc=(sm if i == 3 else None))
            tp_done = sm.n
            C.w("vector", sm, tp_done)
            ovT_dst = ovT4[:].rearrange("p (a t) -> p a t", a=4)[:, :, c * 128:(c + 1) * 128]
            C.w("vector", sx, ov_stored.get(hg - 1, 0))
            C.do("vector", "tensor_copy", out=ovT_dst,
                 in_=PH[:, 0:512].rearrange("p (a z) -> p a z", a=4), inc=sv)
            C.do("vector", "tensor_copy", out=cst[:, 32:36],
                 in_=e2mT[:, 127:512:128], inc=sv)
            for i in range(4):
                C.do("vector", "tensor_scalar", Ktm4[:, i * 128:(i + 1) * 128],
                     PH[:, 512 + i * 128:512 + (i + 1) * 128],
                     cst[:, 32 + i:33 + i], None, ALU.mult,
                     inc=sv, lp=True)
            war["PH_v"] = sv.n
            ktm_done = sv.n
            C.w("tensor", sv, ktm_done)
            C.w("tensor", sv, war.get("PD2_v", 0))
            for i in range(4):
                C.do("tensor", "matmul", PD[:, i * 128:(i + 1) * 128],
                     Ktm4[:, i * 128:(i + 1) * 128], U4[:, i * 128:(i + 1) * 128],
                     start=True, stop=True, inc=(sm if i == 3 else None))
            su_done = sm.n
            C.w("vector", sm, su_done)
            bc4(bcf, col(c, KGC)[:, hg * 4:hg * 4 + 4])
            C.do("vector", "tensor_tensor", t4b[:], S4f[:], bcf[:], ALU.mult, inc=sv)
            C.do("vector", "tensor_tensor", S4f[:], t4b[:], PD[:], ALU.add, inc=sv)
            war["PD2_v"] = sv.n
            C.do("vector", "tensor_copy", out=S4b[:], in_=S4f[:], inc=sv, lp=True)
            war["PD_v"] = sv.n
        ov_done = sv.n
        C.w("sync", sv, ov_done)
        for i, h in enumerate(hs):
            C.do("sync", "dma_start", inc=sx, n=16, out=ovscr[b, h],
                 in_=ovT4[:, i * 2048:(i + 1) * 2048])
        ov_stored[hg] = sx.n
        ld_used = sm.n
        S_used = sm.n
    ph.close()


def emit_block_ffn(C, b, x_in, x_out, kind):
    nc = C.nc
    sx, sw, sm, sv, sa, sy = C.sx, C.sw, C.sm, C.sv, C.sa, C.sy
    isb = kind == "B"
    pre = "f0_" if isb else "f1_"
    wg_d, wu_d, wd_d = C.dram[pre + "wg"], C.dram[pre + "wu"], C.dram[pre + "wd"]

    ph0 = ExitStack()
    xnT = C.sb(ph0, "xnTf", [128, 16 * 2048], BF16)[:]
    C.barrier()
    xnt_ready = emit_ln_to_xnT(C, x_in[b], xnT)

    C.barrier()
    ph = ExitStack()
    wgsl = C.sb(ph, "f1wg", [128, 2 * 2 * 2048], BF16)
    wusl = C.sb(ph, "f1wu", [128, 2 * 2 * 2048], BF16)
    gs = C.sb(ph, "f1g", [128, 2 * 2048], F32)
    us = C.sb(ph, "f1u", [128, 2 * 2048], F32)
    hs = C.sb(ph, "f1h", [128, 2 * 2048], BF16)
    psg = C.ps(ph, "f1psg", [128, 2 * 512], F32)
    psu = C.ps(ph, "f1psu", [128, 2 * 512], F32)
    h_scr = C.dram["h_scr"]
    w_used = [0, 0]
    w_ready = [0, 0]
    w_fixed = [0, 0]
    psg_war = [0, 0]
    psu_war = [0, 0]
    h_stored = 0
    h_prev_done = 0

    def f1_load(sl):
        slot = sl % 2
        C.w("gpsimd", sm, w_used[slot])
        for m in range(2):
            gm = sl * 2 + m
            C.do("gpsimd", "dma_start", inc=C.sd[slot], n=16,
                 out=wgsl[:, slot * 4096 + m * 2048:slot * 4096 + (m + 1) * 2048],
                 in_=wg_d[gm].rearrange("p dc f -> p (dc f)"))
            C.do("gpsimd", "dma_start", inc=C.sd[slot], n=16,
                 out=wusl[:, slot * 4096 + m * 2048:slot * 4096 + (m + 1) * 2048],
                 in_=wu_d[gm].rearrange("p dc f -> p (dc f)"))
        w_ready[slot] = C.sd[slot].n
        if isb:
            C.w("vector", C.sd[slot], w_ready[slot])
            C.w("vector", sm, w_used[slot])
            C.do("vector", "tensor_scalar_add",
                 wgsl[:, slot * 4096:(slot + 1) * 4096],
                 wgsl[:, slot * 4096:(slot + 1) * 4096], -128.0, inc=sv)
            C.do("vector", "tensor_scalar_add",
                 wusl[:, slot * 4096:(slot + 1) * 4096],
                 wusl[:, slot * 4096:(slot + 1) * 4096], -128.0, inc=sv)
            w_fixed[slot] = sv.n

    f1_load(0)
    for sl in range(32):
        if DBG_BAR_F1:
            C.barrier()
        slot = sl % 2
        if sl + 1 < 32:
            f1_load(sl + 1)
        woff = slot * 4096
        for m in range(2):
            for qc in range(4):
                pb = qc % 2
                if m == 0 and qc == 0:
                    if isb:
                        C.w("tensor", sv, w_fixed[slot])
                    else:
                        C.w("tensor", C.sd[slot], w_ready[slot])
                    if sl == 0:
                        C.w("tensor", sv, xnt_ready)
                C.w("tensor", sa, psg_war[pb])
                for dc in range(16):
                    C.do("tensor", "matmul", psg[:, pb * 512:(pb + 1) * 512],
                         wgsl[:, woff + m * 2048 + dc * 128:
                              woff + m * 2048 + (dc + 1) * 128],
                         xnT[:, dc * 2048 + qc * 512:dc * 2048 + (qc + 1) * 512],
                         start=(dc == 0), stop=(dc == 15), inc=(sm if dc == 15 else None))
                g_done = sm.n
                C.w("tensor", sv, psu_war[pb])
                for dc in range(16):
                    C.do("tensor", "matmul", psu[:, pb * 512:(pb + 1) * 512],
                         wusl[:, woff + m * 2048 + dc * 128:
                              woff + m * 2048 + (dc + 1) * 128],
                         xnT[:, dc * 2048 + qc * 512:dc * 2048 + (qc + 1) * 512],
                         start=(dc == 0), stop=(dc == 15), inc=(sm if dc == 15 else None))
                u_done = sm.n
                col = m * 2048 + qc * 512
                C.w("scalar", sm, g_done)
                C.w("scalar", sv, h_prev_done)
                if isb:
                    C.do("scalar", "activation", gs[:, col:col + 512],
                         psg[:, pb * 512:(pb + 1) * 512], AF.Silu,
                         scale=C.cons[pre + "gsc"][:, sl * 2 + m:sl * 2 + m + 1], inc=sa)
                else:
                    C.do("scalar", "activation", gs[:, col:col + 512],
                         psg[:, pb * 512:(pb + 1) * 512], AF.Silu, inc=sa)
                psg_war[pb] = sa.n
                C.w("vector", sm, u_done)
                if isb:
                    C.do("vector", "tensor_scalar",
                         us[:, col:col + 512], psu[:, pb * 512:(pb + 1) * 512],
                         C.cons[pre + "usc"][:, sl * 2 + m:sl * 2 + m + 1], None,
                         ALU.mult, inc=sv)
                else:
                    C.do("vector", "tensor_copy", out=us[:, col:col + 512],
                         in_=psu[:, pb * 512:(pb + 1) * 512], inc=sv)
                psu_war[pb] = sv.n
                C.w("vector", sa, psg_war[pb])
                if m == 0 and qc == 0:
                    C.w("vector", sx, h_stored)
                C.do("vector", "tensor_tensor", hs[:, col:col + 512],
                     gs[:, col:col + 512], us[:, col:col + 512], ALU.mult, inc=sv)
        h_done = sv.n
        h_prev_done = h_done
        C.w("sync", sv, h_done)
        dst = h_scr[sl * 256:(sl + 1) * 256, :].rearrange("(mt p) t -> p mt t", p=128)
        C.do("sync", "dma_start", inc=sx, n=16,
             out=dst, in_=hs[:].rearrange("p (mt t) -> p mt t", mt=2))
        h_stored = sx.n
        w_used[slot] = sm.n
    ph.close()

    ph0.close()

    C.barrier()
    ph = ExitStack()
    if isb:
        build_bc_tile(C, ph, "f0_dscbc", "f0_dsc", 2048)
    h2 = C.sb(ph, "f2h", [128, 64 * 1024], BF16)
    ybf = C.sb(ph, "f2y", [128, 8 * 2048], BF16)
    wd2 = C.sb(ph, "f2wd", [128, 2 * 2048], BF16)
    psY = C.ps(ph, "f2psY", [128, 8 * 512], F32)
    C.bufs["xt3"] = C.sb(ph, "f2xt3", [128, D], F32)[:]
    C.bufs["sqs2"] = C.sb(ph, "f2sqs2", [128, D], BF16)[:]
    h2_used = 0
    yevac = [0] * 8
    wd_used = [0, 0]
    for tc2 in range(2):
        C.w("sync", sm, h2_used)
        for ic in range(64):
            C.do("sync", "dma_start", inc=sy, n=16,
                 out=h2[:, ic * 1024:(ic + 1) * 1024],
                 in_=h_scr[ic * 128:(ic + 1) * 128, tc2 * 1024:(tc2 + 1) * 1024])
        h2_ready = sy.n
        for hc in range(4):
            if DBG_BAR_F2:
                C.barrier()
            for g4 in range(16):
                wb = g4 % 2
                C.w("gpsimd", sm, wd_used[wb])
                C.do("gpsimd", "dma_start", inc=C.sd[wb], n=16,
                     out=wd2[:, wb * 2048:(wb + 1) * 2048].rearrange(
                         "p (i f) -> p i f", i=4),
                     in_=wd_d[g4 * 4:(g4 + 1) * 4][:, :, hc * 512:(hc + 1) * 512]
                     .rearrange("i p f -> p i f"))
                wd_ready = C.sd[wb].n
                wfix = 0
                if isb:
                    C.w("vector", C.sd[wb], wd_ready)
                    C.w("vector", sm, wd_used[wb])
                    C.do("vector", "tensor_scalar_add", wd2[:, wb * 2048:(wb + 1) * 2048],
                         wd2[:, wb * 2048:(wb + 1) * 2048], -128.0, inc=sv)
                    wfix = sv.n
                if g4 == 0:
                    C.w("tensor", sy, h2_ready)
                    for j in range(8):
                        C.w("tensor", sv, yevac[j])
                if isb:
                    C.w("tensor", sv, wfix)
                else:
                    C.w("tensor", C.sd[wb], wd_ready)
                for ic4 in range(4):
                    ic = g4 * 4 + ic4
                    for j in range(8):
                        C.do("tensor", "matmul", psY[:, j * 512:(j + 1) * 512],
                             h2[:, ic * 1024 + j * 128:ic * 1024 + (j + 1) * 128],
                             wd2[:, wb * 2048 + ic4 * 512:wb * 2048 + (ic4 + 1) * 512],
                             start=(ic == 0), stop=(ic == 63),
                             inc=(sm if j == 7 else None))
                wd_used[wb] = sm.n
            sweep_done = sm.n
            C.w("vector", sm, sweep_done)
            if hc == 0:
                C.w("vector", sa, C.war.get("res_sq", 0))
            for j in range(8):
                dsty = ybf[:, j * 2048 + hc * 512:j * 2048 + (hc + 1) * 512]
                if isb:
                    C.do("vector", "tensor_tensor", dsty, psY[:, j * 512:(j + 1) * 512],
                         C.cons["f0_dscbc"][:, hc * 512:(hc + 1) * 512], ALU.mult, inc=sv)
                else:
                    C.do("vector", "tensor_copy", out=dsty,
                         in_=psY[:, j * 512:(j + 1) * 512], inc=sv)
                yevac[j] = sv.n
        y_ready = sv.n
        if "dbg_ybf" in C.dram:
            C.w("sync", sv, y_ready)
            C.do("sync", "dma_start", inc=sx, n=16,
                 out=C.dram["dbg_ybf"][b, tc2], in_=ybf[:])
        for j in range(8):
            i = tc2 * 8 + j
            emit_post_ln_residual(C, ybf[:, j * 2048:(j + 1) * 2048],
                                  x_in[b][i * 128:(i + 1) * 128, :],
                                  x_out[b][i * 128:(i + 1) * 128, :],
                                  0, y_ready)
        h2_used = sm.n
    ph.close()


def build_program(blocks):
    nc = bass.Bass("TRN2", target_bir_lowering=False)
    es = ExitStack()
    C = Ctx(nc, es)
    sx, sw, sm, sv, sa = C.sx, C.sw, C.sm, C.sv, C.sa

    C.dram["x_in"] = nc.dram_tensor("x_in", [NB, T, D], F32, kind="ExternalInput").ap()
    C.dram["x_out"] = nc.dram_tensor("x_out", [NB, T, D], F32, kind="ExternalOutput").ap()

    def din(name, shape, dt):
        C.dram[name] = nc.dram_tensor(name, shape, dt, kind="ExternalInput").ap()
        return C.dram[name]

    din("c_idn", [128, 128], BF16)
    din("c_onesc", [128, 1], BF16)
    din("c_onesr", [1, 128], BF16)
    din("c_onesrf", [1, 128], F32)
    din("c_fcol", [128, 8], F32)
    if "A" in blocks:
        din("wqkv", [3, H, 128, 16, 128], F32)
        din("bqkv", [128, 48], F32)
        din("wout", [16, 128, 2048], F32)
        din("bout", [1, 2048], F32)
        din("c_ropeC", [64, 2048], BF16)
        din("c_ropeS", [64, 2048], BF16)
        C.dram["attnT"] = nc.dram_tensor("attnT", [NB, H, 128, 2048], BF16,
                                         kind="Internal").ap()
    if "B" in blocks or "D" in blocks:
        C.dram["h_scr"] = nc.dram_tensor("h_scr", [FI, T], BF16, kind="Internal").ap()
    if "B" in blocks:
        din("f0_wg", [64, 128, 16, 128], I32)
        din("f0_wu", [64, 128, 16, 128], I32)
        din("f0_wd", [64, 128, 2048], I32)
        din("c_f0_gsc", [128, 64], F32)
        din("c_f0_usc", [128, 64], F32)
        din("f0_dsc", [1, 2048], F32)
    if "D" in blocks:
        din("f1_wg", [64, 128, 16, 128], F32)
        din("f1_wu", [64, 128, 16, 128], F32)
        din("f1_wd", [64, 128, 2048], F32)
    if "C" in blocks:
        din("l1w", [4, H, 128, 16, 128], F32)
        din("l1wo", [16, 128, 2048], F32)
        din("c_wab", [128, 512], F32)
        din("c_bab", [16, 2], F32)
        din("c_cw", [128, 192], F32)
        din("c_cb", [128, 48], F32)
        din("c_selF", [16, 2048], F32)
        din("c_idnf", [128, 128], F32)
        din("c_mLTs", [128, 128], BF16)
        din("c_mUTs", [128, 128], BF16)
        din("c_mUTis", [128, 128], BF16)
        din("c_lnw", [1, 128], F32)
        din("c_lnb", [1, 128], F32)
        C.dram["qkvg_scr"] = nc.dram_tensor("qkvg_scr", [4, H, 128, 2048], BF16,
                                            kind="Internal").ap()
        C.dram["ovT_scr"] = nc.dram_tensor("ovT_scr", [NB, H, 128, 2048], BF16,
                                           kind="Internal").ap()

    B = C.bufs
    B["cols"] = es.enter_context(nc.sbuf_tensor("cols", [128, 16], F32))[:]
    B["cols2"] = es.enter_context(nc.sbuf_tensor("cols2", [128, 16], F32))[:]

    sc = C.sc
    for nm in ("idn", "onesc", "onesr"):
        C.cons[nm] = es.enter_context(
            nc.sbuf_tensor("sb_" + nm, list(C.dram["c_" + nm].shape), BF16))[:]
        C.do("sync", "dma_start", inc=sc, n=16, out=C.cons[nm], in_=C.dram["c_" + nm])
    C.cons["fcol"] = es.enter_context(nc.sbuf_tensor("sb_fcol", [128, 8], F32))[:]
    C.do("sync", "dma_start", inc=sc, n=16, out=C.cons["fcol"], in_=C.dram["c_fcol"])
    C.cons["onesrf"] = es.enter_context(nc.sbuf_tensor("sb_onesrf", [1, 128], F32))[:]
    C.do("sync", "dma_start", inc=sc, n=16, out=C.cons["onesrf"], in_=C.dram["c_onesrf"])
    if "B" in blocks:
        for nm in ("f0_gsc", "f0_usc"):
            C.cons[nm] = es.enter_context(nc.sbuf_tensor("sb_" + nm, [128, 64], F32))[:]
            C.do("sync", "dma_start", inc=sc, n=16, out=C.cons[nm], in_=C.dram["c_" + nm])
    if "C" in blocks:
        for nm, shp, dt_ in (("bab", [16, 2], F32),
                             ("cw", [128, 192], F32), ("cb", [128, 48], F32),
                             ("idnf", [128, 128], F32),
                             ("mLTs", [128, 128], BF16), ("mUTs", [128, 128], BF16),
                             ("mUTis", [128, 128], BF16)):
            C.cons[nm] = es.enter_context(nc.sbuf_tensor("sb_" + nm, shp, dt_))[:]
            C.do("sync", "dma_start", inc=sc, n=16, out=C.cons[nm], in_=C.dram["c_" + nm])
        C.cons["wab"] = es.enter_context(nc.sbuf_tensor("sb_wab", [128, 512], BF16))[:]
        C.do("gpsimd", "dma_start", inc=sc, n=16, out=C.cons["wab"], in_=C.dram["c_wab"])

    if "A" in blocks:
        C.cons["bqkv"] = es.enter_context(nc.sbuf_tensor("sb_bqkv", [128, 48], F32))[:]
        C.do("sync", "dma_start", inc=sc, n=16, out=C.cons["bqkv"], in_=C.dram["bqkv"])
    C.war["consts_sx"] = 0
    for _e in ("tensor", "vector", "scalar", "gpsimd", "sync"):
        C.w(_e, sc, sc.n)



    seq = [blk for blk in "ABCD" if blk in blocks]
    io = [C.dram["x_in"]]
    for i in range(len(seq) - 1):
        io.append(nc.dram_tensor("xmid%d" % i, [NB, T, D], F32, kind="Internal").ap())
    io.append(C.dram["x_out"])
    for idx, blk in enumerate(seq):
        src_ap, dst_ap = io[idx], io[idx + 1]
        for b in range(NB):
            if blk == "A":
                emit_block_a(C, b, src_ap, dst_ap)
            elif blk in ("B", "D"):
                emit_block_ffn(C, b, src_ap, dst_ap, blk)
            elif blk == "C":
                emit_block_c(C, b, src_ap, dst_ap)
            else:
                raise NotImplementedError(blk)

    for s_ in C.sems.values():
        C.w("sync", s_, s_.n)
    C.play()
    es.close()
    return nc


def _consts_a():
    inv = (1.0 / 10000.0 ** (np.arange(0, 64, 2, dtype=np.float32) / 64)).astype(np.float32)
    t = np.arange(T, dtype=np.float32)
    freqs = np.einsum("i,j->ij", t, inv).astype(np.float32)
    cos = np.cos(freqs).astype(np.float32)
    sin = np.sin(freqs).astype(np.float32)
    ropeC = np.ascontiguousarray(np.concatenate([cos, cos], axis=1).T)
    ropeS = np.ascontiguousarray(np.concatenate([-sin, sin], axis=1).T)
    return ropeC.astype(BFNP), ropeS.astype(BFNP)


def prep_shared(inputs, blocks):
    d = {}
    d["c_idn"] = np.eye(128, dtype=BFNP)
    fc = np.zeros((128, 8), dtype=np.float32)
    for k, j in FCOL.items():
        fc[:, j] = FCOL_VALS[k]
    d["c_fcol"] = fc
    d["c_onesc"] = np.ones((128, 1), dtype=BFNP)
    d["c_onesr"] = np.ones((1, 128), dtype=BFNP)
    d["c_onesrf"] = np.ones((1, 128), dtype=np.float32)
    if "A" in blocks:
        w = np.ascontiguousarray(inputs["L0_attn_w_qkv"], dtype=np.float32)
        d["wqkv"] = np.ascontiguousarray(
            w.reshape(16, 128, 3, 16, 128).transpose(2, 3, 1, 0, 4))
        d["bqkv"] = np.ascontiguousarray(
            np.asarray(inputs["L0_attn_b_qkv"], dtype=np.float32)
            .reshape(3, 16, 128).transpose(2, 1, 0).reshape(128, 48))
        d["wout"] = np.ascontiguousarray(
            np.asarray(inputs["L0_attn_w_out"], dtype=np.float32)
            .reshape(16, 128, 16, 128).transpose(2, 1, 0, 3).reshape(16, 128, 2048))
        d["bout"] = np.ascontiguousarray(
            np.asarray(inputs["L0_attn_b_out"], dtype=np.float32).reshape(1, 2048))
        rc, rs = _consts_a()
        d["c_ropeC"] = rc
        d["c_ropeS"] = rs

    def _w_in(w):
        return np.ascontiguousarray(w.T.reshape(16, 128, 64, 128).transpose(2, 1, 0, 3))

    def _w_dn(w):
        return np.ascontiguousarray(np.ascontiguousarray(w).T.reshape(64, 128, 2048))

    def _sc(s):
        return np.ascontiguousarray(np.asarray(s, np.float32).reshape(64, 128).T)

    if "B" in blocks:
        d["f0_wg"] = _w_in(np.asarray(inputs["L0_ffn_gate_w"], np.int32))
        d["f0_wu"] = _w_in(np.asarray(inputs["L0_ffn_up_w"], np.int32))
        d["f0_wd"] = _w_dn(np.asarray(inputs["L0_ffn_down_w"], np.int32))
        d["c_f0_gsc"] = _sc(inputs["L0_ffn_gate_scale"])
        d["c_f0_usc"] = _sc(inputs["L0_ffn_up_scale"])
        d["f0_dsc"] = np.ascontiguousarray(
            np.asarray(inputs["L0_ffn_down_scale"], np.float32).reshape(1, 2048))
    if "C" in blocks:
        def _wh(w):
            return np.asarray(w, np.float32).reshape(16, 128, 16, 128).transpose(2, 1, 0, 3)
        d["l1w"] = np.ascontiguousarray(np.stack([
            _wh(inputs["L1_attn_Wq"]), _wh(inputs["L1_attn_Wk"]),
            _wh(inputs["L1_attn_Wv"]), _wh(inputs["L1_attn_Wg"])]))
        d["l1wo"] = np.ascontiguousarray(
            np.asarray(inputs["L1_attn_Wo"], np.float32)
            .reshape(16, 128, 16, 128).transpose(2, 1, 0, 3).reshape(16, 128, 2048))
        wa = np.asarray(inputs["L1_attn_Wa"], np.float32).reshape(16, 128, 16)
        wb = np.asarray(inputs["L1_attn_Wb"], np.float32).reshape(16, 128, 16)
        d["c_wab"] = np.ascontiguousarray(np.concatenate(
            [wa.transpose(1, 0, 2).reshape(128, 256),
             wb.transpose(1, 0, 2).reshape(128, 256)], axis=1))
        d["c_bab"] = np.ascontiguousarray(np.stack(
            [np.asarray(inputs["L1_attn_ba"], np.float32),
             np.asarray(inputs["L1_attn_bb"], np.float32)], axis=1))
        conv_ws = (inputs["L1_attn_conv_q_w"], inputs["L1_attn_conv_k_w"],
                   inputs["L1_attn_conv_v_w"])
        conv_bs = (inputs["L1_attn_conv_q_b"], inputs["L1_attn_conv_k_b"],
                   inputs["L1_attn_conv_v_b"])
        d["c_cw"] = np.ascontiguousarray(np.concatenate(
            [np.asarray(w_, np.float32).reshape(16, 128, 4)
             .transpose(1, 0, 2).reshape(128, 64) for w_ in conv_ws], axis=1))
        d["c_cb"] = np.ascontiguousarray(np.concatenate(
            [np.asarray(b_, np.float32).reshape(16, 128).T for b_ in conv_bs], axis=1))
        sel = np.zeros((16, 2048), np.float32)
        for hh in range(16):
            sel[hh, hh * 128:(hh + 1) * 128] = 1.0
        d["c_selF"] = sel
        d["c_idnf"] = np.eye(128, dtype=np.float32)
        ii = np.arange(128)
        d["c_mLTs"] = (ii[None, :] < ii[:, None]).astype(BFNP)
        d["c_mUTs"] = (ii[None, :] > ii[:, None]).astype(BFNP)
        d["c_mUTis"] = ((ii[None, :] >= ii[:, None]).astype(np.float32)
                        * np.float32(128.0 ** -0.5)).astype(BFNP)
        d["c_lnw"] = np.asarray(inputs["L1_attn_ln_w"], np.float32).reshape(1, 128)
        d["c_lnb"] = np.asarray(inputs["L1_attn_ln_b"], np.float32).reshape(1, 128)
    if "D" in blocks:
        d["f1_wg"] = _w_in(np.asarray(inputs["L1_ffn_w_gate"], np.float32))
        d["f1_wu"] = _w_in(np.asarray(inputs["L1_ffn_w_up"], np.float32))
        d["f1_wd"] = _w_dn(np.asarray(inputs["L1_ffn_w_down"], np.float32))
    return d


def run_blocks(x_full, inputs, blocks):
    shared = prep_shared(inputs, blocks)
    nc = build_program(blocks)
    in_maps = []
    for c in range(NCORES):
        m = dict(shared)
        m["x_in"] = np.ascontiguousarray(x_full[c * NB:(c + 1) * NB], dtype=np.float32)
        in_maps.append(m)
    res = run_bass_kernel_spmd(nc, in_maps, core_ids=list(range(NCORES)))
    return np.concatenate([r["x_out"] for r in res.results], axis=0)


def kernel(**inputs):
    x = np.ascontiguousarray(np.asarray(inputs["L0_attn_x"], dtype=np.float32))
    return run_blocks(x, inputs, ("A", "B", "C", "D"))
